# version 01 (final)
# baseline (speedup 1.0000x reference)
import numpy as np
from contextlib import ExitStack

import concourse.bass as bass
import concourse.mybir as mybir
from concourse.bass_utils import run_bass_kernel_spmd
from concourse.bass_core_rust import IndirectOffsetOnAxis
import ml_dtypes

F32 = mybir.dt.float32
BF16 = mybir.dt.bfloat16
I32 = mybir.dt.int32
AF = mybir.ActivationFunctionType
OP = mybir.AluOpType
AX = mybir.AxisListType

NCORES = 8
B_TOT, T, D = 32, 2048, 1024
B = B_TOT // NCORES
NTOK = B * T
KD = D // 128

bf16 = ml_dtypes.bfloat16


class Prog:
    def __init__(self, name="mk"):
        self.nc = bass.Bass("TRN2", target_bir_lowering=False, detect_race_conditions=False)
        self.es = ExitStack()
        self._sems = {}
        self.block = None
        self._uid = 0
        self.done = []

    def selfwait(self, eng, instr, name="selfs"):
        tok = self.inc(instr, name, 1)
        eng.wait_ge(self.sem(name), tok)

    def entry_wait(self, eng):
        for s, tok in self.done:
            eng.wait_ge(self.sem(s), tok)

    def set_done(self, conds):
        self.done = list(conds)

    def uid(self):
        self._uid += 1
        return self._uid

    def sem(self, name):
        if name not in self._sems:
            h = self.es.enter_context(self.nc.semaphore(name))
            self._sems[name] = [h, 0]
        return self._sems[name][0]

    def inc(self, instr, name, n=1):
        h = self.sem(name)
        instr.then_inc(h, n)
        self._sems[name][1] += n
        return self._sems[name][1]

    def val(self, name):
        self.sem(name)
        return self._sems[name][1]

    def wait(self, eng, name, tok=None):
        h = self.sem(name)
        eng.wait_ge(h, self._sems[name][1] if tok is None else tok)

    def sb(self, name, shape, dtype, scope=None):
        return (scope or self.es).enter_context(self.nc.sbuf_tensor(f"{name}_{self.uid()}", shape, dtype))

    def ps(self, name, shape, dtype=F32, scope=None):
        return (scope or self.es).enter_context(self.nc.psum_tensor(f"{name}_{self.uid()}", shape, dtype))

    def dram(self, name, shape, dtype, kind="Internal"):
        return self.nc.dram_tensor(name, shape, dtype, kind=kind).ap()

    def close(self):
        self.es.close()


def to_fm(x_btd):
    Bc, Tc, Dc = x_btd.shape
    return np.ascontiguousarray(np.asarray(x_btd).reshape(Bc * Tc, Dc).T)


def from_fm(x_fm, Bc, Tc):
    return np.ascontiguousarray(x_fm.T.reshape(Bc, Tc, x_fm.shape[0]))


def lhsT_tiles(w_din_dout):
    w = np.asarray(w_din_dout)
    din, dout = w.shape
    kk, mm = din // 128, dout // 128
    out = np.empty((128, kk * dout), w.dtype)
    for k in range(kk):
        for m in range(mm):
            out[:, (k * mm + m) * 128:(k * mm + m + 1) * 128] = w[k * 128:(k + 1) * 128, m * 128:(m + 1) * 128]
    return np.ascontiguousarray(out)


def bias_cols(b_vec):
    return np.ascontiguousarray(np.asarray(b_vec).reshape(-1, 128).T)


def consts_host():
    return {
        "ones_col_bf": np.ones((128, 1), bf16),
        "ones_row_bf": np.ones((1, 128), bf16),
        "ones_row": np.ones((1, 128), np.float32),
        "ident_bf": np.eye(128, dtype=bf16),
        "eps_col": np.full((128, 1), 1e-5, np.float32),
        "ones_512_bf": np.ones((1, 512), bf16),
        "trem": np.broadcast_to((T - 1 - np.arange(T, dtype=np.float32)), (128, T)).copy(),
    }


def const_dtype(name):
    if name.endswith("_i32"):
        return I32
    return BF16 if name.endswith("_bf") else F32


def emit_consts(pr, cin):
    blk = pr.block
    C = {}
    for k in cin:
        C[k] = pr.sb(k, list(cin[k].shape), const_dtype(k))

    @blk.sync
    def _(s):
        for k in cin:
            pr.inc(s.dma_start(out=C[k][:, :], in_=cin[k]), "cld", 16)

    ctok = pr.val("cld")
    C["wait"] = lambda eng: eng.wait_ge(pr.sem("cld"), ctok)
    C["ones_row_f"] = C["ones_row"]
    return C


def emit_ln(pr, C, x_hbm, xhat_hbm, tag, d_model=D, ntok=NTOK, slab=1024, eps=1e-5):
    blk = pr.block
    kd = d_model // 128
    nslab = ntok // slab
    sc_ = ExitStack()
    xc = [pr.sb(f"{tag}xc{i}", [128, slab], F32, sc_) for i in range(kd)]
    xb = [pr.sb(f"{tag}xb{i}", [128, slab], BF16, sc_) for i in range(kd)]
    sq = [pr.sb(f"{tag}sq{i}", [128, slab], BF16, sc_) for i in range(kd)]
    stat_ps = pr.ps(f"{tag}stps", [1, 2 * slab], F32, sc_)
    row = pr.sb(f"{tag}row", [1, 5 * slab], F32, sc_)
    rep_ps = pr.ps(f"{tag}repps", [128, 512], F32, sc_)
    rep = pr.sb(f"{tag}rep", [128, 2 * slab], BF16, sc_)
    t1 = pr.sb(f"{tag}t1", [128, slab], BF16, sc_)
    xo = [pr.sb(f"{tag}xo{i}", [128, slab], BF16, sc_) for i in range(2)]
    mu = row[0:1, 0:slab]
    e2 = row[0:1, slab:2 * slab]
    var = row[0:1, 2 * slab:3 * slab]
    lnv = row[0:1, 3 * slab:4 * slab]
    s2 = row[0:1, 4 * slab:5 * slab]
    ld, st, sv, sa, spe = f"{tag}ld", f"{tag}st", f"{tag}v", f"{tag}a", f"{tag}pe"

    for s in range(nslab):
        col0 = s * slab
        ldt = []
        pe_apply_done = pr.val(spe)

        @blk.sync
        def _(sy, col0=col0, ldt=ldt, s=s):
            if s == 0:
                pr.entry_wait(sy)
            if s > 0:
                pr.wait(sy, st)
            for i in range(kd):
                ldt.append(pr.inc(sy.dma_start(out=xc[i][:, :], in_=x_hbm[i * 128:(i + 1) * 128, col0:col0 + slab]), ld, 16))

        @blk.vector
        def _(v, ldt=ldt):
            v.wait_ge(pr.sem(ld), ldt[-1])
            for i in range(kd):
                ii = v.tensor_copy(out=xb[i][:, :], in_=xc[i][:, :])
            pr.inc(ii, sv, 1)
        cast_tok = pr.val(sv)

        @blk.scalar
        def _(sc, ldt=ldt, s=s):
            if s == 0:
                C["wait"](sc)
            sc.wait_ge(pr.sem(ld), ldt[-1])
            for i in range(kd):
                ii = sc.activation(out=sq[i][:, :], in_=xc[i][:, :], func=AF.Square)
            pr.inc(ii, sa, 1)
        sq_tok = pr.val(sa)

        @blk.tensor
        def _(t, s=s):
            if s == 0:
                C["wait"](t)
            t.wait_ge(pr.sem(sv), cast_tok)
            t.wait_ge(pr.sem(sa), sq_tok)
            for n0 in range(0, slab, 512):
                for i in range(kd):
                    t.matmul(stat_ps[0:1, n0:n0 + 512], C["ones_col_bf"][:, 0:1], xb[i][:, n0:n0 + 512],
                             start=(i == 0), stop=(i == kd - 1))
                for i in range(kd):
                    mm = t.matmul(stat_ps[0:1, slab + n0:slab + n0 + 512], C["ones_col_bf"][:, 0:1],
                                  sq[i][:, n0:n0 + 512], start=(i == 0), stop=(i == kd - 1))
            pr.inc(mm, spe, 1)
        stats_tok = pr.val(spe)

        @blk.vector
        def _(v):
            v.wait_ge(pr.sem(spe), stats_tok)
            v.tensor_scalar_mul(mu, stat_ps[0:1, 0:slab], 1.0 / d_model)
            v.tensor_scalar_mul(e2, stat_ps[0:1, slab:2 * slab], 1.0 / d_model)
            v.tensor_mul(out=var, in0=mu, in1=mu)
            pr.inc(v.tensor_sub(out=var, in0=e2, in1=var), sv, 1)
        var_tok = pr.val(sv)

        @blk.scalar
        def _(sc):
            sc.wait_ge(pr.sem(sv), var_tok)
            sc.activation(out=lnv, in_=var, func=AF.Ln, bias=C["eps_col"][0:1, 0:1])
            pr.inc(sc.activation(out=e2, in_=lnv, func=AF.Exp, scale=-0.5), sa, 1)
        rstd_tok = pr.val(sa)

        @blk.vector
        def _(v):
            v.wait_ge(pr.sem(sa), rstd_tok)
            pr.inc(v.scalar_tensor_tensor(out=s2, in0=mu, scalar=-1.0, in1=e2, op0=OP.mult, op1=OP.mult), sv, 1)
        s2_tok = pr.val(sv)

        nrep = slab // 512
        rep_pe_base = pr.val(spe)
        rep_ev_base = pr.val(sv)

        @blk.tensor
        def _(t):
            t.wait_ge(pr.sem(sv), s2_tok)
            for j in range(2 * nrep):
                src = row[0:1, slab + j * 512:slab + (j + 1) * 512] if j < nrep else \
                      row[0:1, 4 * slab + (j - nrep) * 512:4 * slab + (j - nrep + 1) * 512]
                if j > 0:
                    t.wait_ge(pr.sem(sv), rep_ev_base + j)
                pr.inc(t.matmul(rep_ps[:, :], C["ones_row_f"][0:1, :], src, start=True, stop=True), spe, 1)

        @blk.vector
        def _(v):
            for j in range(2 * nrep):
                v.wait_ge(pr.sem(spe), rep_pe_base + j + 1)
                pr.inc(v.tensor_copy(out=rep[:, j * 512:(j + 1) * 512], in_=rep_ps[:, :]), sv, 1)
        rep_tok = pr.val(sv)

        st_base = pr.val(st)
        apply_toks = []

        @blk.vector
        def _(v, col0=col0, st_base=st_base, apply_toks=apply_toks):
            v.wait_ge(pr.sem(sv), rep_tok)
            for i in range(kd):
                if i >= 2:
                    v.wait_ge(pr.sem(st), st_base + i * 16)
                v.tensor_mul(out=t1[:, :], in0=xb[i][:, :], in1=rep[:, 0:slab])
                ii = v.tensor_add(out=xo[i % 2][:, :], in0=t1[:, :], in1=rep[:, slab:2 * slab])
                apply_toks.append(pr.inc(ii, sv, 1))

        @blk.sync
        def _(sy, col0=col0, apply_toks=apply_toks):
            for i in range(kd):
                sy.wait_ge(pr.sem(sv), apply_toks[i])
                pr.inc(sy.dma_start(out=xhat_hbm[i * 128:(i + 1) * 128, col0:col0 + slab], in_=xo[i % 2][:, :]), st, 16)

    @blk.sync
    def _(sy):
        pr.wait(sy, st)
    pr.set_done([(st, pr.val(st))])
    sc_.close()


def emit_gemm(pr, C, tag, xsrc_hbm, w_hbm, out_hbm, din, dout, ntok=NTOK,
              epilogue="copy_bf", bias_hbm=None, bias_row_hbm=None,
              resid_hbm=None, resout_hbm=None, act=None, nch=512):
    blk = pr.block
    kk, mm_ = din // 128, dout // 128
    sc_ = ExitStack()
    wsb = pr.sb(f"{tag}w", [128, kk * dout], BF16, sc_)
    xt = [pr.sb(f"{tag}x{i}", [128, nch], BF16, sc_) for i in range(kk)]
    ps = [pr.ps(f"{tag}ps{i}", [128, nch], F32, sc_) for i in range(2)]
    odt = BF16 if epilogue in ("copy_bf", "act_bias_bf", "bias_bf") else F32
    ob = [pr.sb(f"{tag}ob{i}", [128, nch], odt, sc_) for i in range(2)]
    rb = [pr.sb(f"{tag}rb{i}", [128, nch], F32, sc_) for i in range(mm_)] if resid_hbm is not None else None
    bias_sb = pr.sb(f"{tag}bias", [128, mm_], F32, sc_) if bias_hbm is not None else None
    bias_bf = pr.sb(f"{tag}biasb", [1, dout], BF16, sc_) if bias_row_hbm is not None else None
    ld, st, lw, spe, sev = f"{tag}ld", f"{tag}st", f"{tag}lw", f"{tag}pe", f"{tag}ev"

    lwg, lbg = f"{tag}lwg", f"{tag}lbg"

    @blk.gpsimd
    def _(g):
        pr.entry_wait(g)
        pr.inc(g.dma_start(out=wsb[:, :], in_=w_hbm[:, :]), lwg, 16)
        if bias_bf is not None:
            pr.inc(g.dma_start(out=bias_bf[0:1, :], in_=bias_row_hbm), lbg, 16)

    if bias_sb is not None:
        @blk.sync
        def _(sy):
            pr.entry_wait(sy)
            pr.inc(sy.dma_start(out=bias_sb[:, :], in_=bias_hbm), lw, 16)
    lw_tok = pr.val(lw)
    lwg_tok = pr.val(lwg)
    lbg_tok = pr.val(lbg)

    nnch = ntok // nch
    for n in range(nnch):
        n0 = n * nch
        ldt = []
        rbt = []
        pe_base = pr.val(spe)
        ev_base = pr.val(sev)

        @blk.sync
        def _(sy, n0=n0, ldt=ldt, rbt=rbt, n=n, pe_base=pe_base, ev_base=ev_base):
            if n == 0:
                pr.entry_wait(sy)
            if n > 0:
                pr.wait(sy, spe, pe_base)
                pr.wait(sy, sev, ev_base)
            for i in range(kk):
                ldt.append(pr.inc(sy.dma_start(out=xt[i][:, :], in_=xsrc_hbm[i * 128:(i + 1) * 128, n0:n0 + nch]), ld, 16))
            if rb is not None:
                for m in range(mm_):
                    rbt.append(pr.inc(sy.dma_start(out=rb[m][:, :], in_=resid_hbm[m * 128:(m + 1) * 128, n0:n0 + nch]),
                                      f"{tag}lr", 16))

        @blk.tensor
        def _(t, n=n, ldt=ldt, pe_base=pe_base, ev_base=ev_base):
            if n == 0:
                t.wait_ge(pr.sem(lwg), lwg_tok)
                if bias_bf is not None:
                    t.wait_ge(pr.sem(lbg), lbg_tok)
                C["wait"](t)
            t.wait_ge(pr.sem(ld), ldt[-1])
            for m in range(mm_):
                pp = ps[m % 2]
                if m >= 2 or n > 0:
                    t.wait_ge(pr.sem(sev), ev_base + m - 1)
                for k in range(kk):
                    last = (k == kk - 1) and (bias_bf is None)
                    mmx = t.matmul(pp[:, :], wsb[:, (k * mm_ + m) * 128:(k * mm_ + m + 1) * 128], xt[k][:, :],
                                   start=(k == 0), stop=last)
                if bias_bf is not None:
                    mmx = t.matmul(pp[:, :], bias_bf[0:1, m * 128:(m + 1) * 128], C["ones_512_bf"][0:1, 0:nch],
                                   start=False, stop=True)
                pr.inc(mmx, spe, 1)

        st_base = pr.val(st)
        if epilogue == "copy_bf":
            @blk.vector
            def _(v, pe_base=pe_base, st_base=st_base):
                for m in range(mm_):
                    v.wait_ge(pr.sem(spe), pe_base + m + 1)
                    if m >= 2:
                        v.wait_ge(pr.sem(st), st_base + m * 16)
                    pr.inc(v.tensor_copy(out=ob[m % 2][:, :], in_=ps[m % 2][:, :]), sev, 1)
        elif epilogue == "act_bias_bf":
            @blk.scalar
            def _(scl, pe_base=pe_base, n=n, st_base=st_base):
                if n == 0 and bias_sb is not None:
                    scl.wait_ge(pr.sem(lw), lw_tok)
                for m in range(mm_):
                    scl.wait_ge(pr.sem(spe), pe_base + m + 1)
                    if m >= 2:
                        scl.wait_ge(pr.sem(st), st_base + m * 16)
                    b_ap = bias_sb[:, m:m + 1] if bias_sb is not None else 0.0
                    pr.inc(scl.activation(out=ob[m % 2][:, :], in_=ps[m % 2][:, :], func=act, bias=b_ap), sev, 1)
        elif epilogue == "act_bias_f32":
            @blk.scalar
            def _(scl, pe_base=pe_base, n=n, st_base=st_base):
                if n == 0 and bias_sb is not None:
                    scl.wait_ge(pr.sem(lw), lw_tok)
                for m in range(mm_):
                    scl.wait_ge(pr.sem(spe), pe_base + m + 1)
                    if m >= 2:
                        scl.wait_ge(pr.sem(st), st_base + m * 16)
                    b_ap = bias_sb[:, m:m + 1] if bias_sb is not None else 0.0
                    pr.inc(scl.activation(out=ob[m % 2][:, :], in_=ps[m % 2][:, :], func=act, bias=b_ap), sev, 1)
        elif epilogue == "bias_bf":
            @blk.vector
            def _(v, pe_base=pe_base, st_base=st_base):
                for m in range(mm_):
                    v.wait_ge(pr.sem(spe), pe_base + m + 1)
                    if m >= 2:
                        v.wait_ge(pr.sem(st), st_base + m * 16)
                    pr.inc(v.tensor_copy(out=ob[m % 2][:, :], in_=ps[m % 2][:, :]), sev, 1)
        elif epilogue == "copy_f32":
            @blk.vector
            def _(v, pe_base=pe_base, st_base=st_base):
                for m in range(mm_):
                    v.wait_ge(pr.sem(spe), pe_base + m + 1)
                    if m >= 2:
                        v.wait_ge(pr.sem(st), st_base + m * 16)
                    pr.inc(v.tensor_copy(out=ob[m % 2][:, :], in_=ps[m % 2][:, :]), sev, 1)
        elif epilogue == "bias_f32":
            @blk.vector
            def _(v, pe_base=pe_base, st_base=st_base):
                for m in range(mm_):
                    v.wait_ge(pr.sem(spe), pe_base + m + 1)
                    if m >= 2:
                        v.wait_ge(pr.sem(st), st_base + m * 16)
                    pr.inc(v.tensor_copy(out=ob[m % 2][:, :], in_=ps[m % 2][:, :]), sev, 1)
        elif epilogue == "bias_resid_f32":
            @blk.vector
            def _(v, pe_base=pe_base, rbt=rbt, st_base=st_base):
                for m in range(mm_):
                    v.wait_ge(pr.sem(spe), pe_base + m + 1)
                    v.wait_ge(pr.sem(f"{tag}lr"), rbt[-1])
                    if m >= 2:
                        v.wait_ge(pr.sem(st), st_base + m * 16)
                    pr.inc(v.tensor_add(out=ob[m % 2][:, :], in0=ps[m % 2][:, :], in1=rb[m][:, :]), sev, 1)
        else:
            raise ValueError(epilogue)

        dest = out_hbm if resout_hbm is None else resout_hbm

        @blk.sync
        def _(sy, n0=n0, ev_base=ev_base):
            for m in range(mm_):
                sy.wait_ge(pr.sem(sev), ev_base + m + 1)
                pr.inc(sy.dma_start(out=dest[m * 128:(m + 1) * 128, n0:n0 + nch], in_=ob[m % 2][:, :]), st, 16)

    @blk.sync
    def _(sy):
        pr.wait(sy, st)
    pr.set_done([(st, pr.val(st))])
    sc_.close()


def emit_program_end(pr, sems):
    blk = pr.block

    @blk.sync
    def _(sy):
        for s in sems:
            pr.wait(sy, s)


def emit_time_interp(pr, C, xhat_hbm, out_hbm, eps, tag, d_model=D, nseq=B, tlen=T):
    blk = pr.block
    kd = d_model // 128
    sc_ = ExitStack()
    xt = [pr.sb(f"{tag}x{i}", [128, tlen], BF16, sc_) for i in range(2)]
    ot = [pr.sb(f"{tag}o{i}", [128, tlen], BF16, sc_) for i in range(2)]
    ld, st, sv = f"{tag}ld", f"{tag}st", f"{tag}v"
    tiles = [(s, i) for s in range(nseq) for i in range(kd)]
    st_base0 = pr.val(st)
    sv_base0 = pr.val(sv)
    ldt = []

    @blk.sync
    def _(sy):
        pr.entry_wait(sy)
        for idx, (s, i) in enumerate(tiles):
            if idx >= 2:
                s2, i2 = tiles[idx - 2]
                pr.wait(sy, sv, sv_base0 + idx - 1)
                pr.inc(sy.dma_start(out=out_hbm[i2 * 128:(i2 + 1) * 128, s2 * tlen:(s2 + 1) * tlen],
                                    in_=ot[idx % 2][:, :]), st, 16)
            ldt.append(pr.inc(sy.dma_start(out=xt[idx % 2][:, :],
                                           in_=xhat_hbm[i * 128:(i + 1) * 128, s * tlen:(s + 1) * tlen]),
                              f"{tag}ld{idx % 2}", 16))
        for idx in (len(tiles) - 2, len(tiles) - 1):
            s2, i2 = tiles[idx]
            pr.wait(sy, sv, sv_base0 + idx + 1)
            pr.inc(sy.dma_start(out=out_hbm[i2 * 128:(i2 + 1) * 128, s2 * tlen:(s2 + 1) * tlen],
                                in_=ot[idx % 2][:, :]), st, 16)

    @blk.vector
    def _(v):
        for idx, (s, i) in enumerate(tiles):
            x_ = xt[idx % 2]
            o_ = ot[idx % 2]
            v.wait_ge(pr.sem(f"{tag}ld{idx % 2}"), ldt[idx])
            if idx >= 2:
                v.wait_ge(pr.sem(st), st_base0 + idx * 16)
            v.tensor_scalar_mul(o_[:, 1:tlen], x_[:, 0:tlen - 1], float(eps))
            v.tensor_scalar_mul(o_[:, 0:1], x_[:, 1:2], float(1.0 - eps))
            pr.inc(v.scalar_tensor_tensor(out=o_[:, 1:tlen - 1], in0=x_[:, 2:tlen], scalar=float(1.0 - eps),
                                          in1=o_[:, 1:tlen - 1], op0=OP.mult, op1=OP.add), sv, 1)

    @blk.sync
    def _(sy):
        pr.wait(sy, st)
    pr.set_done([(st, pr.val(st))])
    sc_.close()


def emit_wkv_core(pr, C, k_hbm, v_hbm, sigr_hbm, out_hbm, td_hbm, u_hbm, tag,
                  d_model=D, nseq=B, tlen=T):
    blk = pr.block
    kd = d_model // 128
    sc_ = ExitStack()
    kt = pr.sb(f"{tag}k", [128, tlen], F32, sc_)
    vt = pr.sb(f"{tag}v", [128, tlen], F32, sc_)
    rt = pr.sb(f"{tag}r", [128, tlen], BF16, sc_)
    e_ = pr.sb(f"{tag}e", [128, tlen], F32, sc_)
    ev = pr.sb(f"{tag}ev", [128, tlen], F32, sc_)
    ce = pr.sb(f"{tag}ce", [128, tlen], F32, sc_)
    cev = pr.sb(f"{tag}cev", [128, tlen], F32, sc_)
    tmp = pr.sb(f"{tag}tmp", [128, tlen], F32, sc_)
    ab = pr.sb(f"{tag}ab", [128, tlen], F32, sc_)
    bb = pr.sb(f"{tag}bb", [128, tlen], F32, sc_)
    ot = pr.sb(f"{tag}ot", [128, tlen], BF16, sc_)
    zer = pr.sb(f"{tag}zer", [128, tlen], F32, sc_)
    wcol = pr.sb(f"{tag}w", [128, kd], F32, sc_)
    ucol = pr.sb(f"{tag}u", [128, kd], F32, sc_)
    ld, st, sv, sa = f"{tag}ld", f"{tag}st", f"{tag}v_", f"{tag}a"

    @blk.sync
    def _(sy):
        pr.entry_wait(sy)
        pr.inc(sy.dma_start(out=wcol[:, :], in_=td_hbm), ld, 16)
        pr.inc(sy.dma_start(out=ucol[:, :], in_=u_hbm), ld, 16)
    prep_tok = pr.val(ld)

    @blk.vector
    def _(v):
        v.wait_ge(pr.sem(ld), prep_tok)
        v.memset(zer[:, :], 0.0)
        pr.inc(v.tensor_relu(out=wcol[:, :], in_=wcol[:, :]), sv, 1)
    prep_v = pr.val(sv)

    tiles = [(s, i) for s in range(nseq) for i in range(kd)]
    for idx, (s, i) in enumerate(tiles):
        ldt = []

        @blk.sync
        def _(sy, s=s, i=i, idx=idx, ldt=ldt):
            if idx > 0:
                pr.wait(sy, st)
            ldt.append(pr.inc(sy.dma_start(out=kt[:, :], in_=k_hbm[i * 128:(i + 1) * 128, s * tlen:(s + 1) * tlen]), ld, 16))
            ldt.append(pr.inc(sy.dma_start(out=vt[:, :], in_=v_hbm[i * 128:(i + 1) * 128, s * tlen:(s + 1) * tlen]), ld, 16))
            ldt.append(pr.inc(sy.dma_start(out=rt[:, :], in_=sigr_hbm[i * 128:(i + 1) * 128, s * tlen:(s + 1) * tlen]), ld, 16))

        @blk.vector
        def _(v, i=i, idx=idx, ldt=ldt):
            if idx == 0:
                v.wait_ge(pr.sem(sv), prep_v)
                C["wait"](v)
            v.wait_ge(pr.sem(ld), ldt[-1])
            v.tensor_scalar(out=tmp[:, :], in0=C["trem"][:, :], scalar1=wcol[:, i:i + 1], scalar2=None, op0=OP.mult)
            v.tensor_sub(out=e_[:, :], in0=kt[:, :], in1=tmp[:, :])
            pr.inc(v.tensor_scalar(out=e_[:, :], in0=e_[:, :], scalar1=10.0, scalar2=-20.0, op0=OP.min, op1=OP.max), sv, 1)
            pr.inc(v.tensor_scalar(out=tmp[:, :], in0=kt[:, :], scalar1=ucol[:, i:i + 1], scalar2=10.0, op0=OP.add, op1=OP.min), sv, 1)
            pr.inc(v.tensor_scalar_max(tmp[:, :], tmp[:, :], -20.0), sv, 1)
        ein_tok = pr.val(sv)

        @blk.scalar
        def _(sc, idx=idx):
            sc.wait_ge(pr.sem(sv), ein_tok)
            pr.inc(sc.activation(out=e_[:, :], in_=e_[:, :], func=AF.Exp), sa, 1)
            pr.inc(sc.activation(out=tmp[:, :], in_=tmp[:, :], func=AF.Exp), sa, 1)
        exp_tok = pr.val(sa)

        @blk.vector
        def _(v, idx=idx):
            v.wait_ge(pr.sem(sa), exp_tok)
            v.tensor_mul(out=ev[:, :], in0=e_[:, :], in1=vt[:, :])
            v.tensor_tensor_scan(out=ce[:, :], data0=zer[:, :], data1=e_[:, :], initial=0.0, op0=OP.add, op1=OP.add)
            v.tensor_tensor_scan(out=cev[:, :], data0=zer[:, :], data1=ev[:, :], initial=0.0, op0=OP.add, op1=OP.add)
            v.tensor_mul(out=ab[:, :], in0=tmp[:, :], in1=vt[:, :])
            v.tensor_add(out=ab[:, 1:tlen], in0=ab[:, 1:tlen], in1=cev[:, 0:tlen - 1])
            v.tensor_copy(out=bb[:, :], in_=tmp[:, :])
            pr.inc(v.tensor_add(out=bb[:, 1:tlen], in0=bb[:, 1:tlen], in1=ce[:, 0:tlen - 1]), sv, 1)
        ab_tok = pr.val(sv)

        @blk.scalar
        def _(sc, idx=idx):
            sc.wait_ge(pr.sem(sv), ab_tok)
            sc.activation(out=bb[:, :], in_=bb[:, :], func=AF.Ln)
            pr.inc(sc.activation(out=bb[:, :], in_=bb[:, :], func=AF.Exp, scale=-1.0), sa, 1)
        rec_tok = pr.val(sa)

        stt = []

        @blk.vector
        def _(v, idx=idx, stt=stt):
            v.wait_ge(pr.sem(sa), rec_tok)
            v.tensor_mul(out=ab[:, :], in0=ab[:, :], in1=bb[:, :])
            pr.inc(v.tensor_mul(out=ot[:, :], in0=ab[:, :], in1=rt[:, :]), sv, 1)

        @blk.sync
        def _(sy, s=s, i=i):
            pr.wait(sy, sv)
            pr.inc(sy.dma_start(out=out_hbm[i * 128:(i + 1) * 128, s * tlen:(s + 1) * tlen], in_=ot[:, :]), st, 16)

    @blk.sync
    def _(sy):
        pr.wait(sy, st)
    pr.set_done([(st, pr.val(st))])
    sc_.close()


def emit_dx(pr, C, xhat_hbm, x_hbm, d_hbm, out_hbm, tag, d_model=D, ntok=NTOK, slab=2048):
    blk = pr.block
    kd = d_model // 128
    nslab = ntok // slab
    sc_ = ExitStack()
    xb = pr.sb(f"{tag}xb", [128, slab], BF16, sc_)
    xf = pr.sb(f"{tag}xf", [128, slab], F32, sc_)
    zo = [pr.sb(f"{tag}z{i}", [128, slab], F32, sc_) for i in range(2)]
    dcol = pr.sb(f"{tag}d", [128, kd], F32, sc_)
    ld, st, sv = f"{tag}ld", f"{tag}st", f"{tag}v"

    @blk.sync
    def _(sy):
        pr.entry_wait(sy)
        pr.inc(sy.dma_start(out=dcol[:, :], in_=d_hbm), ld, 16)

    tiles = [(s, i) for s in range(nslab) for i in range(kd)]
    st_base0 = pr.val(st)
    for idx, (s, i) in enumerate(tiles):
        @blk.sync
        def _(sy, s=s, i=i, idx=idx):
            if idx > 0:
                pr.wait(sy, sv)
            pr.inc(sy.dma_start(out=xb[:, :], in_=xhat_hbm[i * 128:(i + 1) * 128, s * slab:(s + 1) * slab]), ld, 16)
            pr.inc(sy.dma_start(out=xf[:, :], in_=x_hbm[i * 128:(i + 1) * 128, s * slab:(s + 1) * slab]), ld, 16)
        ld_tok = pr.val(ld)

        @blk.vector
        def _(v, i=i, idx=idx, ld_tok=ld_tok):
            v.wait_ge(pr.sem(ld), ld_tok)
            if idx >= 2:
                v.wait_ge(pr.sem(st), st_base0 + idx * 16)
            pr.inc(v.scalar_tensor_tensor(out=zo[idx % 2][:, :], in0=xb[:, :], scalar=dcol[:, i:i + 1],
                                          in1=xf[:, :], op0=OP.mult, op1=OP.add), sv, 1)
        v_tok = pr.val(sv)

        @blk.sync
        def _(sy, s=s, i=i, idx=idx, v_tok=v_tok):
            sy.wait_ge(pr.sem(sv), v_tok)
            pr.inc(sy.dma_start(out=out_hbm[i * 128:(i + 1) * 128, s * slab:(s + 1) * slab], in_=zo[idx % 2][:, :]), st, 16)

    @blk.sync
    def _(sy):
        pr.wait(sy, st)
    pr.set_done([(st, pr.val(st))])
    sc_.close()


def emit_rnn_scan(pr, C, bx_hbm, at_hbm, hall_hbm, tag, nseq=B, tlen=T, chunk=128):
    blk = pr.block
    JB = 8
    W = JB * nseq
    CW = W * chunk
    nchunk = tlen // chunk
    sc_ = ExitStack()
    at_bf = pr.sb(f"{tag}at", [128, JB * 1024], BF16, sc_)
    bxb = [pr.sb(f"{tag}bx{i}", [128, CW], F32, sc_) for i in range(2)]
    hist = [pr.sb(f"{tag}hi{i}", [128, CW], BF16, sc_) for i in range(2)]
    hT = pr.sb(f"{tag}hT", [128, W], BF16, sc_)
    tmp = pr.sb(f"{tag}tmp", [128, W], F32, sc_)
    p_h = pr.ps(f"{tag}ph", [128, W], F32, sc_)
    atld, sP, sV, sA = f"{tag}atld", f"{tag}P", f"{tag}V", f"{tag}A"
    bxld = [f"{tag}bxl{i}" for i in range(2)]
    hst = [f"{tag}hs{i}" for i in range(2)]

    @blk.gpsimd
    def _(g):
        pr.entry_wait(g)
        pr.inc(g.dma_start(out=at_bf[:, :], in_=at_hbm), atld, 16)

    def bx_src(c, s):
        return bx_hbm.rearrange("(j p) t -> p j t", p=128)[:, :, s * tlen + c * chunk:s * tlen + (c + 1) * chunk]

    def hall_dst(c, s):
        return hall_hbm.rearrange("(j p) t -> p j t", p=128)[:, :, s * tlen + c * chunk:s * tlen + (c + 1) * chunk]

    def sb_chunk_view(t_):
        return t_[:, :].rearrange("p (j s i) -> p j s i", j=JB, s=nseq, i=chunk)

    def sb_seq_view(t_, s):
        return sb_chunk_view(t_)[:, :, s, :]

    @blk.sync
    def _(sy):
        pr.entry_wait(sy)
        for c in range(nchunk):
            if c >= 1:
                sy.wait_ge(pr.sem(sA), 1 + c * chunk)
                for s_ in range(nseq):
                    pr.inc(sy.dma_start(out=hall_dst(c - 1, s_), in_=sb_seq_view(hist[(c - 1) % 2], s_)),
                           hst[(c - 1) % 2], 16)
            if c >= 2:
                sy.wait_ge(pr.sem(sV), (c - 1) * chunk)
            for s_ in range(nseq):
                pr.inc(sy.dma_start(out=sb_seq_view(bxb[c % 2], s_), in_=bx_src(c, s_)), bxld[c % 2], 16)
        sy.wait_ge(pr.sem(sA), 1 + nchunk * chunk)
        for s_ in range(nseq):
            pr.inc(sy.dma_start(out=hall_dst(nchunk - 1, s_), in_=sb_seq_view(hist[(nchunk - 1) % 2], s_)),
                   hst[(nchunk - 1) % 2], 16)

    @blk.vector
    def _(v):
        pr.inc(v.memset(hT[:, :], 0.0), sA, 1)
        PAIR = 2 * chunk
        with v.Fori(0, nchunk // 2) as c2:
            for par in range(2):
                v.wait_ge(pr.sem(bxld[par]), c2 * (16 * nseq) + 16 * nseq)
                bxv = sb_chunk_view(bxb[par])
                with v.Fori(0, chunk) as i:
                    v.wait_ge(pr.sem(sP), c2 * PAIR + (i + (par * chunk + 1)))
                    v.tensor_tensor(out=tmp[:, :], in0=p_h[:, :], in1=bxv[:, :, :, bass.ds(i, 1)],
                                    op=OP.add).then_inc(pr.sem(sV), 1)
        pr._sems[sV][1] += tlen

    @blk.tensor
    def _(t):
        t.wait_ge(pr.sem(atld), 16)
        PAIR = 2 * chunk
        with t.Fori(0, nchunk // 2) as c2:
            for par in range(2):
                with t.Fori(0, chunk) as i:
                    t.wait_ge(pr.sem(sA), c2 * PAIR + (i + (par * chunk + 1)))
                    for j in range(JB):
                        for k in range(JB):
                            mm = t.matmul(p_h[:, j * nseq:(j + 1) * nseq],
                                          at_bf[:, (k * JB + j) * 128:(k * JB + j + 1) * 128],
                                          hT[:, k * nseq:(k + 1) * nseq],
                                          start=(k == 0), stop=(k == JB - 1))
                            if j == JB - 1 and k == JB - 1:
                                mm.then_inc(pr.sem(sP), 1)
        pr._sems[sP][1] += tlen

    @blk.scalar
    def _(sc):
        PAIR = 2 * chunk
        with sc.Fori(0, nchunk // 2) as c2:
            for par in range(2):
                sc.wait_ge(pr.sem(hst[par]), c2 * (16 * nseq))
                hv = sb_chunk_view(hist[par])
                with sc.Fori(0, chunk) as i:
                    sc.wait_ge(pr.sem(sV), c2 * PAIR + (i + (par * chunk + 1)))
                    sc.activation(out=hT[:, :], in_=tmp[:, :], func=AF.Tanh)
                    sc.activation(out=hv[:, :, :, bass.ds(i, 1)], in_=tmp[:, :], func=AF.Tanh).then_inc(pr.sem(sA), 1)
        pr._sems[sA][1] += tlen

    @blk.sync
    def _(sy):
        for i in range(2):
            pr.wait(sy, hst[i])
    pr.set_done([(hst[0], pr.val(hst[0])), (hst[1], pr.val(hst[1])), (sV, pr.val(sV))])
    sc_.close()


def build_l1_ffn(pr, C, io, x_in, x_out, scr, ntok=NTOK):
    emit_ln(pr, C, x_in, scr["xhat"], "lf" + scr["u"], ntok=ntok)
    emit_gemm(pr, C, "f1" + scr["u"], scr["xhat"], io["w1"], scr["h4k"], D, 4096, ntok=ntok,
              epilogue="act_bias_bf", bias_hbm=io["b1c"], act=AF.Gelu)
    emit_gemm(pr, C, "f2" + scr["u"], scr["h4k"], io["w2"], None, 4096, D, ntok=ntok,
              epilogue="bias_resid_f32", bias_row_hbm=io["b2r"],
              resid_hbm=x_in, resout_hbm=x_out)


def build_l1_attn_b(pr, C, io, x_in, x_out, scr, ntok=NTOK, nseq=B, tlen=T):
    u = scr["u"]
    emit_ln(pr, C, x_in, scr["xhat"], "lb" + u, ntok=ntok)
    import math
    _EPS = math.exp(-1.0 / 12.0)
    EPS_R, EPS_K, EPS_V = _EPS / 2.0, _EPS + 0.3 / 11.0, _EPS
    emit_time_interp(pr, C, scr["xhat"], scr["xir"], EPS_R, "tr" + u, nseq=nseq, tlen=tlen)
    emit_time_interp(pr, C, scr["xhat"], scr["xik"], EPS_K, "tk" + u, nseq=nseq, tlen=tlen)
    emit_time_interp(pr, C, scr["xhat"], scr["xiv"], EPS_V, "tv" + u, nseq=nseq, tlen=tlen)
    emit_gemm(pr, C, "gr" + u, scr["xir"], io["wr"], scr["sigr"], D, D, ntok=ntok,
              epilogue="act_bias_bf", bias_hbm=io["brc"], act=AF.Sigmoid)
    emit_gemm(pr, C, "gk" + u, scr["xik"], io["wk"], scr["kf"], D, D, ntok=ntok,
              epilogue="bias_f32", bias_row_hbm=io["bkr"])
    emit_gemm(pr, C, "gv" + u, scr["xiv"], io["wv"], scr["vf"], D, D, ntok=ntok,
              epilogue="bias_f32", bias_row_hbm=io["bvr"])
    emit_wkv_core(pr, C, scr["kf"], scr["vf"], scr["sigr"], scr["wo_in"], io["tdc"], io["uc"], "wk" + u,
                  nseq=nseq, tlen=tlen)
    emit_gemm(pr, C, "go" + u, scr["wo_in"], io["wo"], None, D, D, ntok=ntok,
              epilogue="bias_resid_f32", bias_row_hbm=io["bor"],
              resid_hbm=x_in, resout_hbm=x_out)


def build_l1_attn_a(pr, C, io, x_in, x_out, scr, ntok=NTOK, nseq=B, tlen=T, chunk=128):
    u = scr["u"]
    emit_ln(pr, C, x_in, scr["xhat"], "la" + u, ntok=ntok)
    emit_gemm(pr, C, "gb" + u, scr["xhat"], io["bw"], scr["bx"], D, D, ntok=ntok, epilogue="copy_f32")
    emit_rnn_scan(pr, C, scr["bx"], io["at"], scr["hall"], "rn" + u, nseq=nseq, tlen=tlen, chunk=chunk)
    emit_dx(pr, C, scr["xhat"], x_in, io["dc"], scr["resid2"], "dx" + u, ntok=ntok, slab=min(2048, ntok))
    emit_gemm(pr, C, "gy" + u, scr["hall"], io["cw"], None, D, D, ntok=ntok,
              epilogue="bias_resid_f32", resid_hbm=scr["resid2"], resout_hbm=x_out)



NH, HD = 16, 64
VW = NH * (HD + 1)


def attn_consts_host(tlen=T):
    inv = 1.0 / 10000.0 ** (np.arange(0, HD, 2) / HD)
    freq = np.concatenate([inv, inv])
    t = np.arange(tlen, dtype=np.float64)
    cos_d = np.cos(t[None, :] * freq[:, None])
    sin_d = np.sin(t[None, :] * freq[:, None])
    sig = np.where(np.arange(HD) % 2 == 0, -1.0, 1.0)
    c = {}
    c["ropecos"] = np.tile(cos_d, (NH, 1)).astype(np.float32)
    c["ropesin"] = np.tile(sin_d * sig[:, None], (NH, 1)).astype(np.float32)
    tk = np.arange(128)[:, None]
    tq = np.arange(512)[None, :]
    for i, off in enumerate([0, 128, 256, 384]):
        mb = (tk + off <= tq)
        c[f"maskb{i}_bf"] = mb.astype(bf16)
        c[f"maska{i}_bf"] = (~mb).astype(bf16)
    c["iotacols"] = np.ascontiguousarray(
        np.arange(tlen, dtype=np.float32).reshape(-1, 128).T)
    c["qiota16"] = np.broadcast_to(np.arange(tlen, dtype=np.float32), (16, tlen)).copy()
    hs = np.zeros((KD, 128, NH), np.float32)
    for ch in range(KD):
        for p in range(128):
            hs[ch, p, (ch * 128 + p) // HD] = 1.0
    c["headsel_bf"] = np.ascontiguousarray(hs.reshape(KD * 128, NH)).astype(bf16)
    c["headselT"] = np.ascontiguousarray(hs.transpose(0, 2, 1).reshape(KD * 16, 128))
    c["c5ln2"] = np.full((128, 1), -5.0 * np.log(2.0), np.float32)
    return c


def perm_rope(wT):
    din, dout = wT.shape
    pidx = np.arange(dout).reshape(NH, HD)
    out = pidx.copy()
    out[:, 0::2] = pidx[:, 1::2]
    out[:, 1::2] = pidx[:, 0::2]
    return np.ascontiguousarray(wT[:, out.reshape(-1)])


def perm_rope_bias(b):
    p = np.arange(D).reshape(NH, HD)
    out = p.copy()
    out[:, 0::2] = p[:, 1::2]
    out[:, 1::2] = p[:, 0::2]
    return np.ascontiguousarray(np.asarray(b).reshape(-1)[out.reshape(-1)].reshape(1, D))


def vw_ext(wvT, bv):
    w = np.zeros((D, VW), np.float32)
    b = np.zeros(VW, np.float32)
    for h in range(NH):
        w[:, h * (HD + 1):h * (HD + 1) + HD] = wvT[:, h * HD:(h + 1) * HD]
        b[h * (HD + 1):h * (HD + 1) + HD] = bv[h * HD:(h + 1) * HD]
        b[h * (HD + 1) + HD] = 1.0
    return np.ascontiguousarray(w), np.ascontiguousarray(b.reshape(1, VW))


def emit_gemm_tm(pr, C, tag, xsrc_hbm, wt_hbm, bias_row_hbm, out_hbm, din, dout,
                 ntok=NTOK, nch=512):
    blk = pr.block
    kk = din // 128
    mm_ = ntok // 128
    nn = (dout + nch - 1) // nch
    sc_ = ExitStack()
    wsb = pr.sb(f"{tag}w", [128, kk * dout], BF16, sc_)
    bias_bf = pr.sb(f"{tag}b", [1, dout], BF16, sc_)
    xt = [pr.sb(f"{tag}x{i}", [128, 128], BF16, sc_) for i in range(kk)]
    ps = [pr.ps(f"{tag}ps{i}", [128, nch], F32, sc_) for i in range(2)]
    ob = [pr.sb(f"{tag}ob{i}", [128, nch], BF16, sc_) for i in range(2)]
    ld, st, lwg, lbg, spe, sev = (f"{tag}ld", f"{tag}st", f"{tag}lwg", f"{tag}lbg",
                                  f"{tag}pe", f"{tag}ev")

    @blk.gpsimd
    def _(g):
        pr.entry_wait(g)
        pr.inc(g.dma_start(out=wsb[:, :].rearrange("p (k n) -> p k n", k=kk),
                           in_=wt_hbm.rearrange("(k p) n -> p k n", p=128)), lwg, 16)
        pr.inc(g.dma_start(out=bias_bf[0:1, :], in_=bias_row_hbm), lbg, 16)
    lwg_tok, lbg_tok = pr.val(lwg), pr.val(lbg)

    for m in range(mm_):
        ldt = []
        pe_base, ev_base, st_base = pr.val(spe), pr.val(sev), pr.val(st)

        @blk.sync
        def _(sy, m=m, ldt=ldt, pe_base=pe_base):
            if m == 0:
                pr.entry_wait(sy)
            else:
                pr.wait(sy, spe, pe_base)
            for k in range(kk):
                ldt.append(pr.inc(sy.dma_start(out=xt[k][:, :],
                                               in_=xsrc_hbm[k * 128:(k + 1) * 128, m * 128:(m + 1) * 128]), ld, 16))

        @blk.tensor
        def _(t, m=m, ldt=ldt, ev_base=ev_base):
            if m == 0:
                t.wait_ge(pr.sem(lwg), lwg_tok)
                t.wait_ge(pr.sem(lbg), lbg_tok)
                C["wait"](t)
            t.wait_ge(pr.sem(ld), ldt[-1])
            for n in range(nn):
                n0 = n * nch
                nw = min(nch, dout - n0)
                if n >= 2 or m > 0:
                    t.wait_ge(pr.sem(sev), ev_base + n - 1)
                for k in range(kk):
                    t.matmul(ps[n % 2][:, 0:nw], xt[k][:, :], wsb[:, k * dout + n0:k * dout + n0 + nw],
                             start=(k == 0), stop=False)
                mmx = t.matmul(ps[n % 2][:, 0:nw], C["ones_row_bf"][0:1, :], bias_bf[0:1, n0:n0 + nw],
                               start=False, stop=True)
                pr.inc(mmx, spe, 1)

        @blk.vector
        def _(v, m=m, pe_base=pe_base, st_base=st_base):
            for n in range(nn):
                nw = min(nch, dout - n * nch)
                v.wait_ge(pr.sem(spe), pe_base + n + 1)
                if n >= 2:
                    v.wait_ge(pr.sem(st), st_base + n * 16)
                pr.inc(v.tensor_copy(out=ob[n % 2][:, 0:nw], in_=ps[n % 2][:, 0:nw]), sev, 1)

        @blk.sync
        def _(sy, m=m, ev_base=ev_base):
            for n in range(nn):
                n0 = n * nch
                nw = min(nch, dout - n0)
                sy.wait_ge(pr.sem(sev), ev_base + n + 1)
                pr.inc(sy.dma_start(out=out_hbm[m * 128:(m + 1) * 128, n0:n0 + nw], in_=ob[n % 2][:, 0:nw]), st, 16)

    @blk.sync
    def _(sy):
        pr.wait(sy, st)
    pr.set_done([(st, pr.val(st))])
    sc_.close()


def emit_rope(pr, C, qc_hbm, qp_hbm, cos_hbm, sin_hbm, out_hbm, tag, nseq=B, tlen=T):
    blk = pr.block
    sc_ = ExitStack()
    a = [pr.sb(f"{tag}a{i}", [128, tlen], BF16, sc_) for i in range(2)]
    b_ = [pr.sb(f"{tag}b{i}", [128, tlen], BF16, sc_) for i in range(2)]
    cosc = pr.sb(f"{tag}c", [128, tlen], F32, sc_)
    sinc = pr.sb(f"{tag}s", [128, tlen], F32, sc_)
    t1 = pr.sb(f"{tag}t1", [128, tlen], F32, sc_)
    t2 = pr.sb(f"{tag}t2", [128, tlen], F32, sc_)
    ot = [pr.sb(f"{tag}o{i}", [128, tlen], BF16, sc_) for i in range(2)]
    st, sv, lc = f"{tag}st", f"{tag}v", f"{tag}lc"
    tiles = [(i, s) for i in range(KD) for s in range(nseq)]
    st_base0, sv_base0 = pr.val(st), pr.val(sv)
    ldt, lct = [], []

    @blk.sync
    def _(sy):
        pr.entry_wait(sy)
        for idx, (i, s) in enumerate(tiles):
            if idx >= 2:
                i2, s2 = tiles[idx - 2]
                pr.wait(sy, sv, sv_base0 + idx - 1)
                pr.inc(sy.dma_start(out=out_hbm[i2 * 128:(i2 + 1) * 128, s2 * tlen:(s2 + 1) * tlen],
                                    in_=ot[idx % 2][:, :]), st, 16)
            if s == 0:
                if idx > 0:
                    pr.wait(sy, sv, sv_base0 + idx)
                lct.append(pr.inc(sy.dma_start(out=cosc[:, :], in_=cos_hbm[i * 128:(i + 1) * 128, :]), lc, 16))
                pr.inc(sy.dma_start(out=sinc[:, :], in_=sin_hbm[i * 128:(i + 1) * 128, :]), lc, 16)
                lct[-1] = pr.val(lc)
            else:
                lct.append(lct[-1])
            ldt.append(pr.inc(sy.dma_start(out=a[idx % 2][:, :],
                                           in_=qc_hbm[i * 128:(i + 1) * 128, s * tlen:(s + 1) * tlen]), f"{tag}l{idx % 2}", 16))
            pr.inc(sy.dma_start(out=b_[idx % 2][:, :],
                                in_=qp_hbm[i * 128:(i + 1) * 128, s * tlen:(s + 1) * tlen]), f"{tag}l{idx % 2}", 16)
            ldt[-1] = pr.val(f"{tag}l{idx % 2}")
        for idx in (len(tiles) - 2, len(tiles) - 1):
            i2, s2 = tiles[idx]
            pr.wait(sy, sv, sv_base0 + idx + 1)
            pr.inc(sy.dma_start(out=out_hbm[i2 * 128:(i2 + 1) * 128, s2 * tlen:(s2 + 1) * tlen],
                                in_=ot[idx % 2][:, :]), st, 16)

    @blk.vector
    def _(v):
        for idx, (i, s) in enumerate(tiles):
            v.wait_ge(pr.sem(f"{tag}l{idx % 2}"), ldt[idx])
            v.wait_ge(pr.sem(lc), lct[idx])
            if idx >= 2:
                v.wait_ge(pr.sem(st), st_base0 + idx * 16)
            v.tensor_mul(out=t1[:, :], in0=a[idx % 2][:, :], in1=cosc[:, :])
            v.tensor_mul(out=t2[:, :], in0=b_[idx % 2][:, :], in1=sinc[:, :])
            pr.inc(v.tensor_add(out=ot[idx % 2][:, :], in0=t1[:, :], in1=t2[:, :]), sv, 1)

    @blk.sync
    def _(sy):
        pr.wait(sy, st)
    pr.set_done([(st, pr.val(st))])
    sc_.close()


def emit_pdld(pr, C, xhat_hbm, wdT_hbm, bd_hbm, ld_scr, eq_scr, tag, nseq=B, tlen=T):
    blk = pr.block
    KTI = tlen // 128
    ekldc = pr.sb(f"{tag}ekldc", [128, nseq * NH * KTI], F32)
    sc_ = ExitStack()
    xm = pr.sb(f"{tag}xm", [128, KD * nseq], F32, sc_)
    xmn_bf = pr.sb(f"{tag}xmb", [128, KD * nseq], BF16, sc_)
    xt = pr.sb(f"{tag}xt", [128, tlen], BF16, sc_)
    wd = pr.sb(f"{tag}wd", [128, KD * NH], BF16, sc_)
    bdr = pr.sb(f"{tag}bdr", [1, NH], BF16, sc_)
    pdps = pr.ps(f"{tag}ps", [nseq, NH], F32, sc_)
    pdt = pr.sb(f"{tag}pd", [nseq, 3 * NH], F32, sc_)
    ldrow = pr.sb(f"{tag}lr", [1, nseq * NH], F32, sc_)
    ldcols = pr.sb(f"{tag}lcl", [NH, nseq], F32, sc_)
    reps = pr.ps(f"{tag}rps", [128, nseq * NH], F32, sc_)
    ldrep = pr.sb(f"{tag}lrep", [128, nseq * NH], F32, sc_)
    q16 = pr.sb(f"{tag}q16", [NH, nseq * tlen], F32, sc_)
    kldc = pr.sb(f"{tag}kldc", [128, nseq * NH * KTI], F32, sc_)
    ld, st, sv, sa, spe, lwg = (f"{tag}ld_", f"{tag}st", f"{tag}v", f"{tag}a",
                                f"{tag}pe", f"{tag}lwg")

    @blk.gpsimd
    def _(g):
        pr.entry_wait(g)
        pr.inc(g.dma_start(out=wd[:, :].rearrange("p (k n) -> p k n", k=KD),
                           in_=wdT_hbm.rearrange("(k p) n -> p k n", p=128)), lwg, 16)
        pr.inc(g.dma_start(out=bdr[0:1, :], in_=bd_hbm), f"{tag}lbg", 16)
    wd_tok = pr.val(lwg)
    bd_tok = pr.val(f"{tag}lbg")

    for i in range(KD):
        for s in range(nseq):
            @blk.sync
            def _(sy, i=i, s=s):
                if i == 0 and s == 0:
                    pr.entry_wait(sy)
                else:
                    pr.wait(sy, sv)
                pr.inc(sy.dma_start(out=xt[:, :], in_=xhat_hbm[i * 128:(i + 1) * 128, s * tlen:(s + 1) * tlen]), ld, 16)
            tok = pr.val(ld)

            @blk.vector
            def _(v, i=i, s=s, tok=tok):
                v.wait_ge(pr.sem(ld), tok)
                pr.inc(v.reduce_sum(xm[:, i * nseq + s:i * nseq + s + 1], xt[:, :], axis=AX.X), sv, 1)

    @blk.vector
    def _(v):
        pr.wait(v, sv)
        pr.selfwait(v, v.tensor_scalar_mul(xm[:, :], xm[:, :], 1.0 / tlen))
        pr.inc(v.tensor_copy(out=xmn_bf[:, :], in_=xm[:, :]), sv, 1)
    xm_tok = pr.val(sv)

    @blk.tensor
    def _(t):
        t.wait_ge(pr.sem(sv), xm_tok)
        t.wait_ge(pr.sem(lwg), wd_tok)
        t.wait_ge(pr.sem(f"{tag}lbg"), bd_tok)
        C["wait"](t)
        for i in range(KD):
            t.matmul(pdps[:, :], xmn_bf[:, i * nseq:(i + 1) * nseq], wd[:, i * NH:(i + 1) * NH],
                     start=(i == 0), stop=False)
        mm = t.matmul(pdps[:, :], C["ones_row_bf"][0:1, 0:nseq], bdr[0:1, :], start=False, stop=True)
        pr.inc(mm, spe, 1)
    pd_tok = pr.val(spe)

    pd_ = pdt[:, 0:NH]
    e2_ = pdt[:, NH:2 * NH]
    ld_ = pdt[:, 2 * NH:3 * NH]

    @blk.scalar
    def _(sc):
        sc.wait_ge(pr.sem(spe), pd_tok)
        C["wait"](sc)
        pr.selfwait(sc, sc.activation(out=pd_, in_=pdps[:, :], func=AF.Sigmoid))
        pr.inc(sc.activation(out=e2_, in_=pd_, func=AF.Exp,
                             scale=float(-8.0 * np.log(2.0)), bias=C["c5ln2"][0:nseq, 0:1]), sa, 2)

    @blk.vector
    def _(v):
        pr.wait(v, sa)
        pr.inc(v.tensor_scalar(out=e2_, in0=e2_, scalar1=-1.0, scalar2=1.0, op0=OP.mult, op1=OP.add), sv, 1)
    om_tok = pr.val(sv)

    @blk.scalar
    def _(sc):
        sc.wait_ge(pr.sem(sv), om_tok)
        pr.inc(sc.activation(out=ld_, in_=e2_, func=AF.Ln), sa, 1)
    ld_tok = pr.val(sa)

    @blk.sync
    def _(sy):
        sy.wait_ge(pr.sem(sa), ld_tok)
        pr.inc(sy.dma_start(out=ld_scr, in_=ld_), st, 16)
        pr.wait(sy, st)
        pr.inc(sy.dma_start(out=ldrow[0:1, :], in_=ld_scr.rearrange("s h -> (s h)")[None, :]), ld, 16)
        pr.inc(sy.dma_start(out=ldcols[:, :], in_=ld_scr.rearrange("s h -> h s"),
                            allow_slow_non_contiguous=True), ld, 16)
    rows_tok = pr.val(ld)

    @blk.tensor
    def _(t):
        t.wait_ge(pr.sem(ld), rows_tok)
        pr.inc(t.matmul(reps[:, :], C["ones_row"][0:1, :], ldrow[0:1, :], start=True, stop=True), spe, 1)
    rep_tok = pr.val(spe)

    @blk.vector
    def _(v):
        v.wait_ge(pr.sem(spe), rep_tok)
        pr.inc(v.tensor_copy(out=ldrep[:, :], in_=reps[:, :]), sv, 1)
        for s in range(nseq):
            v.tensor_scalar(out=q16[:, s * tlen:(s + 1) * tlen], in0=C["qiota16"][:, :],
                            scalar1=ldcols[:, s:s + 1], scalar2=None, op0=OP.mult)
        for s in range(nseq):
            for h in range(NH):
                idx = s * NH + h
                v.tensor_scalar(out=kldc[:, idx * KTI:(idx + 1) * KTI], in0=C["iotacols"][:, :],
                                scalar1=ldrep[:, idx:idx + 1], scalar2=None, op0=OP.mult)
        pr.inc(v.engine_nop(), sv, 1)
    qk_tok = pr.val(sv)

    @blk.scalar
    def _(sc):
        sc.wait_ge(pr.sem(sv), qk_tok)
        pr.inc(sc.activation(out=q16[:, :], in_=q16[:, :], func=AF.Exp, scale=-1.0), sa, 1)
        pr.inc(sc.activation(out=ekldc[:, :], in_=kldc[:, :], func=AF.Exp), sa, 1)
    eq_tok = pr.val(sa)

    @blk.sync
    def _(sy):
        sy.wait_ge(pr.sem(sa), eq_tok)
        for s in range(nseq):
            pr.inc(sy.dma_start(out=eq_scr[s], in_=q16[:, s * tlen:(s + 1) * tlen]), st, 16)
        pr.wait(sy, st)
    pr.set_done([(st, pr.val(st)), (sa, pr.val(sa))])
    sc_.close()
    return ekldc


def emit_attn_core(pr, C, qr_hbm, kr_hbm, vtm_hbm, eq_scr, ekldc, o_hbm, tag,
                   nseq=B, tlen=T):
    blk = pr.block
    KTI = tlen // 128
    NQC = tlen // 512
    sc_ = ExitStack()
    vt = [pr.sb(f"{tag}v{i}", [128, VW], BF16, sc_) for i in range(KTI)]
    kh = [pr.sb(f"{tag}kh{i}", [64, tlen], BF16, sc_) for i in range(2)]
    qh = [pr.sb(f"{tag}qh{i}", [64, tlen], BF16, sc_) for i in range(2)]
    eqr = [pr.sb(f"{tag}eq{i}", [1, tlen], F32, sc_) for i in range(2)]
    pbuf = [pr.sb(f"{tag}pb{i}", [128, 1024], BF16, sc_) for i in range(2)]
    pfin = [pr.sb(f"{tag}pf{i}", [128, 512], BF16, sc_) for i in range(8)]
    pxt = pr.sb(f"{tag}pxt", [128, 512], BF16, sc_)
    pxa = [pr.sb(f"{tag}px{i}", [128, 512], BF16, sc_) for i in range(4)]
    eqc = pr.sb(f"{tag}eqc", [65, 512], F32, sc_)
    tca = pr.sb(f"{tag}tca", [65, 512], F32, sc_)
    ocmb = pr.sb(f"{tag}ocm", [65, 512], F32, sc_)
    rcp = pr.sb(f"{tag}rcp", [1, 512], F32, sc_)
    obuf = [pr.sb(f"{tag}ob{i}", [64, 512], BF16, sc_) for i in range(2)]
    S_ps = pr.ps(f"{tag}sps", [128, 2048], F32, sc_)
    acc_b = pr.ps(f"{tag}acb", [65, 512], F32, sc_)
    acc_a = pr.ps(f"{tag}aca", [65, 512], F32, sc_)
    rep_ps = pr.ps(f"{tag}rps", [65, 512], F32, sc_)
    ape, aact, adve, ast = f"{tag}pe", f"{tag}act", f"{tag}dve", f"{tag}st"
    avl, akl, aeq = f"{tag}vl", f"{tag}kl", f"{tag}eql"
    for nm in (ape, aact, adve, ast, avl, akl, aeq):
        pr.sem(nm)

    def cls(kt, j):
        if kt < 4 * j:
            return "b"
        if kt <= 4 * j + 3:
            return "x"
        return "a"

    base = {k: pr.val(n) for k, n in
            (("pe", ape), ("act", aact), ("dve", adve), ("st", ast),
             ("vl", avl), ("kl", akl), ("eq", aeq))}
    APH = NQC * (KTI // 2 + 1)

    def act_exp_tok(s, h, j, p):
        return base["act"] + (s * NH + h) * APH + j * (KTI // 2 + 1) + p + 1

    def act_rcp_tok(s, h, j):
        return base["act"] + (s * NH + h) * APH + j * (KTI // 2 + 1) + KTI // 2 + 1

    TOK = {}
    c = dict(base)
    slice_exp = {0: 0, 1: 0, 2: 0, 3: 0}
    pfin_free = {i: 0 for i in range(8)}
    pxa_free = {i: 0 for i in range(4)}
    pbuf_free = {0: 0, 1: 0}
    accfree = {"b": 0, "a": 0}
    repfree = [0]
    obuf_free = {0: 0, 1: 0}
    khfree = {0: 0, 1: 0}
    vfree = [0]

    for s in range(nseq):
        TOK[("vfree", s)] = vfree[0]
        c["vl"] += 16 * KTI
        TOK[("vl", s)] = c["vl"]
        for h in range(NH):
            TOK[("khfree", s, h)] = khfree[h % 2]
            c["kl"] += 32
            c["eq"] += 16
            TOK[("kl", s, h)] = c["kl"]
            TOK[("eq", s, h)] = c["eq"]
            for j in range(NQC):
                for kt in range(KTI):
                    sl = kt % 4
                    if kt < 4:
                        TOK[("slfree", s, h, j, kt)] = slice_exp[sl]
                    else:
                        TOK[("slfree", s, h, j, kt)] = act_exp_tok(s, h, j, (kt - 4) // 2)
                    c["pe"] += 1
                    TOK[("smm", s, h, j, kt)] = c["pe"]
                omm_tok = {}
                cp = c["pe"]
                first = {"b": True, "a": True}
                for kt in range(KTI):
                    cc = cls(kt, j)
                    for part in (("b",) if cc == "b" else ("a",) if cc == "a" else ("b", "a")):
                        if first[part]:
                            TOK[("accfree", s, h, j, part)] = accfree[part]
                            first[part] = False
                        cp += 1
                        omm_tok[(kt, part)] = cp
                for p in range(KTI // 2):
                    TOK[("pbfree", s, h, j, p)] = pbuf_free[p % 2]
                    c["act"] += 1
                    assert c["act"] == act_exp_tok(s, h, j, p)
                    slice_exp[(2 * p) % 4] = c["act"]
                    slice_exp[(2 * p) % 4 + 1] = c["act"]
                    for kt in (2 * p, 2 * p + 1):
                        cc = cls(kt, j)
                        TOK[("pffree", s, h, j, kt)] = pfin_free[kt % 8]
                        if cc == "x":
                            c["dve"] += 1
                            TOK[("dve", s, h, j, kt, "b")] = c["dve"]
                            TOK[("pxfree", s, h, j, kt)] = pxa_free[kt - 4 * j]
                            c["dve"] += 2
                            TOK[("dve", s, h, j, kt, "a")] = c["dve"]
                        else:
                            c["dve"] += 1
                            TOK[("dve", s, h, j, kt, cc)] = c["dve"]
                        for part in (("b",) if cc == "b" else ("a",) if cc == "a" else ("b", "a")):
                            if cc == "x" and part == "a":
                                pxa_free[kt - 4 * j] = omm_tok[(kt, part)]
                            else:
                                pfin_free[kt % 8] = omm_tok[(kt, part)]
                        if kt % 2 == 1:
                            pbuf_free[p % 2] = c["dve"]
                c["pe"] = cp
                TOK[("repfree", s, h, j)] = repfree[0]
                c["pe"] += 1
                TOK[("rep1", s, h, j)] = c["pe"]
                c["dve"] += 3
                TOK[("ocmb", s, h, j)] = c["dve"]
                accfree["a"] = c["dve"]
                accfree["b"] = c["dve"]
                c["act"] += 1
                assert c["act"] == act_rcp_tok(s, h, j)
                c["pe"] += 1
                TOK[("rep2", s, h, j)] = c["pe"]
                TOK[("obfree", s, h, j)] = obuf_free[j % 2]
                c["dve"] += 1
                TOK[("store", s, h, j)] = c["dve"]
                repfree[0] = c["dve"]
                c["st"] += 16
                obuf_free[j % 2] = c["st"]
            khfree[h % 2] = c["pe"]
        vfree[0] = c["pe"]

    for s in range(nseq):
        @blk.sync
        def _(sy, s=s):
            if s == 0:
                pr.entry_wait(sy)
            else:
                sy.wait_ge(pr.sem(ape), TOK[("vfree", s)])
            for i in range(KTI):
                sy.dma_start(out=vt[i][:, :],
                             in_=vtm_hbm[(s * KTI + i) * 128:(s * KTI + i + 1) * 128, :]
                             ).then_inc(pr.sem(avl), 16)

        for h in range(NH):
            @blk.sync
            def _(sy, s=s, h=h):
                if TOK[("khfree", s, h)]:
                    sy.wait_ge(pr.sem(ape), TOK[("khfree", s, h)])
                sy.dma_start(out=kh[h % 2][:, :],
                             in_=kr_hbm[h * 64:(h + 1) * 64, s * tlen:(s + 1) * tlen]
                             ).then_inc(pr.sem(akl), 16)
                sy.dma_start(out=qh[h % 2][:, :],
                             in_=qr_hbm[h * 64:(h + 1) * 64, s * tlen:(s + 1) * tlen]
                             ).then_inc(pr.sem(akl), 16)
                sy.dma_start(out=eqr[h % 2][0:1, :], in_=eq_scr[s][h:h + 1, :]
                             ).then_inc(pr.sem(aeq), 16)

            @blk.tensor
            def _(t, s=s, h=h):
                if s == 0 and h == 0:
                    C["wait"](t)
                t.wait_ge(pr.sem(akl), TOK[("kl", s, h)])
                for j in range(NQC):
                    q0 = j * 512
                    for kt in range(KTI):
                        sl = kt % 4
                        if TOK[("slfree", s, h, j, kt)]:
                            t.wait_ge(pr.sem(aact), TOK[("slfree", s, h, j, kt)])
                        t.matmul(S_ps[:, sl * 512:(sl + 1) * 512],
                                 kh[h % 2][:, kt * 128:(kt + 1) * 128],
                                 qh[h % 2][:, q0:q0 + 512], start=True, stop=True
                                 ).then_inc(pr.sem(ape), 1)
                    t.wait_ge(pr.sem(avl), TOK[("vl", s)])
                    first = {"b": True, "a": True}
                    for kt in range(KTI):
                        cc = cls(kt, j)
                        vsl = vt[kt][:, h * 65:(h + 1) * 65]
                        for part in (("b",) if cc == "b" else ("a",) if cc == "a" else ("b", "a")):
                            t.wait_ge(pr.sem(adve), TOK[("dve", s, h, j, kt, part)])
                            if first[part]:
                                if TOK[("accfree", s, h, j, part)]:
                                    t.wait_ge(pr.sem(adve), TOK[("accfree", s, h, j, part)])
                            if cc == "x" and part == "a":
                                src = pxa[kt - 4 * j]
                            else:
                                src = pfin[kt % 8]
                            acc = acc_b if part == "b" else acc_a
                            last = (kt == min(4 * j + 3, KTI - 1)) if part == "b" else (kt == KTI - 1)
                            t.matmul(acc[:, :], vsl, src[:, :], start=first[part], stop=last
                                     ).then_inc(pr.sem(ape), 1)
                            first[part] = False
                    t.wait_ge(pr.sem(aeq), TOK[("eq", s, h)])
                    if TOK[("repfree", s, h, j)]:
                        t.wait_ge(pr.sem(adve), TOK[("repfree", s, h, j)])
                    t.matmul(rep_ps[:, :], C["ones_row"][0:1, 0:65], eqr[h % 2][0:1, q0:q0 + 512],
                             start=True, stop=True).then_inc(pr.sem(ape), 1)
                    t.wait_ge(pr.sem(aact), act_rcp_tok(s, h, j))
                    t.matmul(rep_ps[0:64, :], C["ones_row"][0:1, 0:64], rcp[0:1, :],
                             start=True, stop=True).then_inc(pr.sem(ape), 1)

            @blk.scalar
            def _(sc, s=s, h=h):
                if s == 0 and h == 0:
                    C["wait"](sc)
                for j in range(NQC):
                    for p in range(KTI // 2):
                        if TOK[("pbfree", s, h, j, p)]:
                            sc.wait_ge(pr.sem(adve), TOK[("pbfree", s, h, j, p)])
                        sc.wait_ge(pr.sem(ape), TOK[("smm", s, h, j, 2 * p + 1)])
                        sl = (2 * p) % 4
                        sc.activation(out=pbuf[p % 2][:, :], in_=S_ps[:, sl * 512:(sl + 2) * 512],
                                      func=AF.Exp).then_inc(pr.sem(aact), 1)
                    sc.wait_ge(pr.sem(adve), TOK[("ocmb", s, h, j)])
                    sc.activation(out=rcp[0:1, :], in_=ocmb[64:65, :], func=AF.Ln)
                    sc.activation(out=rcp[0:1, :], in_=rcp[0:1, :], func=AF.Exp, scale=-1.0
                                  ).then_inc(pr.sem(aact), 1)

            @blk.vector
            def _(v, s=s, h=h):
                ek_base = (s * NH + h) * KTI
                for j in range(NQC):
                    for kt in range(KTI):
                        cc = cls(kt, j)
                        p = kt // 2
                        v.wait_ge(pr.sem(aact), act_exp_tok(s, h, j, p))
                        if TOK[("pffree", s, h, j, kt)]:
                            v.wait_ge(pr.sem(ape), TOK[("pffree", s, h, j, kt)])
                        half = pbuf[p % 2][:, (kt % 2) * 512:(kt % 2 + 1) * 512]
                        ekcol = ekldc[:, ek_base + kt:ek_base + kt + 1]
                        if cc == "b":
                            v.tensor_copy(out=pfin[kt % 8][:, :], in_=half).then_inc(pr.sem(adve), 1)
                        elif cc == "a":
                            v.tensor_scalar(out=pfin[kt % 8][:, :], in0=half, scalar1=ekcol,
                                            scalar2=None, op0=OP.mult).then_inc(pr.sem(adve), 1)
                        else:
                            off = kt - 4 * j
                            v.tensor_tensor(out=pfin[kt % 8][:, :], in0=half,
                                            in1=C[f"maskb{off}_bf"][:, :], op=OP.mult
                                            ).then_inc(pr.sem(adve), 1)
                            v.tensor_scalar(out=pxt[:, :], in0=half, scalar1=ekcol,
                                            scalar2=None, op0=OP.mult)
                            if TOK[("pxfree", s, h, j, kt)]:
                                v.wait_ge(pr.sem(ape), TOK[("pxfree", s, h, j, kt)])
                            v.tensor_tensor(out=pxa[kt - 4 * j][:, :], in0=pxt[:, :],
                                            in1=C[f"maska{off}_bf"][:, :], op=OP.mult
                                            ).then_inc(pr.sem(adve), 2)
                    v.wait_ge(pr.sem(ape), TOK[("rep1", s, h, j)])
                    v.tensor_copy(out=eqc[:, :], in_=rep_ps[:, :])
                    v.tensor_tensor(out=tca[:, :], in0=acc_a[:, :], in1=eqc[:, :], op=OP.mult)
                    v.tensor_add(out=ocmb[:, :], in0=tca[:, :], in1=acc_b[:, :]
                                 ).then_inc(pr.sem(adve), 3)
                    v.wait_ge(pr.sem(ape), TOK[("rep2", s, h, j)])
                    if TOK[("obfree", s, h, j)]:
                        v.wait_ge(pr.sem(ast), TOK[("obfree", s, h, j)])
                    v.tensor_tensor(out=obuf[j % 2][:, :], in0=ocmb[0:64, :],
                                    in1=rep_ps[0:64, :], op=OP.mult).then_inc(pr.sem(adve), 1)

            @blk.sync
            def _(sy, s=s, h=h):
                for j in range(NQC):
                    sy.wait_ge(pr.sem(adve), TOK[("store", s, h, j)])
                    sy.dma_start(out=o_hbm[h * 64:(h + 1) * 64,
                                           s * tlen + j * 512:s * tlen + (j + 1) * 512],
                                 in_=obuf[j % 2][:, :]).then_inc(pr.sem(ast), 16)

    for k, n in (("pe", ape), ("act", aact), ("dve", adve), ("st", ast),
                 ("vl", avl), ("kl", akl), ("eq", aeq)):
        pr._sems[n][1] = c[k]

    @blk.sync
    def _(sy):
        pr.wait(sy, ast)
    pr.set_done([(ast, pr.val(ast)), (ape, pr.val(ape)), (adve, pr.val(adve)),
                 (aact, pr.val(aact))])
    sc_.close()


def emit_gn_gate(pr, C, o_hbm, g_hbm, gnw_hbm, gnb_hbm, stat_scr, out_hbm, tag,
                 nseq=B, tlen=T, eps=1e-5, dbg_hbm=None):
    blk = pr.block
    sc_ = ExitStack()
    ot = pr.sb(f"{tag}o", [128, tlen], BF16, sc_)
    gt = pr.sb(f"{tag}g", [128, tlen], BF16, sc_)
    sq = pr.sb(f"{tag}sq", [128, tlen], BF16, sc_)
    r12 = pr.sb(f"{tag}r", [128, 2 * KD], F32, sc_)
    r12b = pr.sb(f"{tag}rb", [128, 2 * KD], BF16, sc_)
    hsel = pr.sb(f"{tag}hs", [128, KD * NH], BF16, sc_)
    hselT = pr.sb(f"{tag}hsT", [16, KD * 128], F32, sc_)
    sps = pr.ps(f"{tag}ps", [1, 2 * NH], F32, sc_)
    srow = pr.sb(f"{tag}sr", [1, 8 * NH], F32, sc_)
    mcol = pr.sb(f"{tag}mc", [NH, 2], F32, sc_)
    reps = pr.ps(f"{tag}rps", [128, 2], F32, sc_)
    repc = pr.sb(f"{tag}rc", [128, 2 * KD], F32, sc_)
    gw = pr.sb(f"{tag}gw", [128, KD], F32, sc_)
    gb = pr.sb(f"{tag}gb", [128, KD], F32, sc_)
    ob = [pr.sb(f"{tag}ob{i}", [128, tlen], BF16, sc_) for i in range(2)]
    ld, st, sv, sa, spe = f"{tag}ld", f"{tag}st", f"{tag}v", f"{tag}a", f"{tag}pe"

    @blk.sync
    def _(sy):
        pr.entry_wait(sy)
        pr.inc(sy.dma_start(out=hselT[:, :].rearrange("h (c p) -> h c p", c=KD),
                            in_=pr.attn_hselT_hbm.rearrange("(c h) p -> h c p", h=NH)), ld, 16)
        pr.inc(sy.dma_start(out=hsel[:, :].rearrange("p (c h) -> p c h", c=KD),
                            in_=pr.attn_hsel_hbm.rearrange("(c p) h -> p c h", p=128)), ld, 16)
        pr.inc(sy.dma_start(out=gw[:, :], in_=gnw_hbm), ld, 16)
        pr.inc(sy.dma_start(out=gb[:, :], in_=gnb_hbm), ld, 16)
    hs_tok = pr.val(ld)

    for s in range(nseq):
        for ch in range(KD):
            @blk.sync
            def _(sy, s=s, ch=ch):
                if s + ch > 0:
                    pr.wait(sy, sv)
                    pr.wait(sy, spe)
                pr.inc(sy.dma_start(out=ot[:, :], in_=o_hbm[ch * 128:(ch + 1) * 128, s * tlen:(s + 1) * tlen]), ld, 16)
            tok = pr.val(ld)

            @blk.scalar
            def _(sc, s=s, ch=ch, tok=tok):
                sc.wait_ge(pr.sem(ld), tok)
                pr.inc(sc.activation(out=sq[:, :], in_=ot[:, :], func=AF.Square), sa, 1)
            sq_tok = pr.val(sa)

            @blk.vector
            def _(v, s=s, ch=ch, tok=tok, sq_tok=sq_tok):
                v.wait_ge(pr.sem(ld), tok)
                v.reduce_sum(r12[:, ch:ch + 1], ot[:, :], axis=AX.X)
                v.wait_ge(pr.sem(sa), sq_tok)
                v.reduce_sum(r12[:, KD + ch:KD + ch + 1], sq[:, :], axis=AX.X)
                v.tensor_copy(out=r12b[:, ch:ch + 1], in_=r12[:, ch:ch + 1])
                pr.inc(v.tensor_copy(out=r12b[:, KD + ch:KD + ch + 1], in_=r12[:, KD + ch:KD + ch + 1]), sv, 1)
        red_tok = pr.val(sv)

        @blk.tensor
        def _(t, s=s, red_tok=red_tok):
            if s == 0:
                t.wait_ge(pr.sem(ld), hs_tok)
                C["wait"](t)
            t.wait_ge(pr.sem(sv), red_tok)
            for ch in range(KD):
                t.matmul(sps[0:1, 0:NH], r12b[:, ch:ch + 1], hsel[:, ch * NH:(ch + 1) * NH],
                         start=(ch == 0), stop=(ch == KD - 1))
            for ch in range(KD):
                mm = t.matmul(sps[0:1, NH:2 * NH], r12b[:, KD + ch:KD + ch + 1], hsel[:, ch * NH:(ch + 1) * NH],
                              start=(ch == 0), stop=(ch == KD - 1))
            pr.inc(mm, spe, 1)
        st_tok = pr.val(spe)

        mu_ = srow[0:1, 0:NH]
        e2_ = srow[0:1, NH:2 * NH]
        var_ = srow[0:1, 2 * NH:3 * NH]
        lnv_ = srow[0:1, 3 * NH:4 * NH]
        rstd_ = srow[0:1, 4 * NH:5 * NH]
        mrs_ = srow[0:1, 5 * NH:6 * NH]

        @blk.vector
        def _(v, st_tok=st_tok):
            v.wait_ge(pr.sem(spe), st_tok)
            v.tensor_scalar_mul(mu_, sps[0:1, 0:NH], 1.0 / (HD * tlen))
            pr.selfwait(v, v.tensor_scalar_mul(e2_, sps[0:1, NH:2 * NH], 1.0 / (HD * tlen)))
            pr.selfwait(v, v.tensor_mul(out=var_, in0=mu_, in1=mu_))
            pr.inc(v.tensor_sub(out=var_, in0=e2_, in1=var_), sv, 1)
        var_tok = pr.val(sv)

        @blk.scalar
        def _(sc, var_tok=var_tok):
            sc.wait_ge(pr.sem(sv), var_tok)
            pr.selfwait(sc, sc.activation(out=lnv_, in_=var_, func=AF.Ln, bias=C["eps_col"][0:1, 0:1]))
            pr.inc(sc.activation(out=rstd_, in_=lnv_, func=AF.Exp, scale=-0.5), sa, 1)
        rs_tok = pr.val(sa)

        @blk.vector
        def _(v, rs_tok=rs_tok):
            v.wait_ge(pr.sem(sa), rs_tok)
            pr.inc(v.tensor_mul(out=mrs_, in0=mu_, in1=rstd_), sv, 1)
        mrs_tok = pr.val(sv)

        @blk.sync
        def _(sy, s=s, mrs_tok=mrs_tok):
            sy.wait_ge(pr.sem(sv), mrs_tok)
            if dbg_hbm is not None:
                pr.inc(sy.dma_start(out=dbg_hbm[s], in_=srow[0:1, :]), st, 16)
            pr.inc(sy.dma_start(out=stat_scr[s], in_=srow[0:1, 4 * NH:6 * NH]), st, 16)
            pr.wait(sy, st)
            pr.inc(sy.dma_start(out=mcol[:, :],
                                in_=stat_scr[s].rearrange("o (a h) -> (o h) a", a=2),
                                allow_slow_non_contiguous=True), ld, 16)
        mc_tok = pr.val(ld)

        for ch in range(KD):
            @blk.tensor
            def _(t, ch=ch, mc_tok=mc_tok):
                t.wait_ge(pr.sem(ld), mc_tok)
                if ch > 0:
                    pr.wait(t, sv)
                pr.inc(t.matmul(reps[:, :], hselT[:, ch * 128:(ch + 1) * 128], mcol[:, :],
                                start=True, stop=True), spe, 1)
            rp_tok = pr.val(spe)

            @blk.vector
            def _(v, ch=ch, rp_tok=rp_tok):
                v.wait_ge(pr.sem(spe), rp_tok)
                v.tensor_copy(out=repc[:, 2 * ch:2 * ch + 1], in_=reps[:, 0:1])
                pr.inc(v.tensor_copy(out=repc[:, 2 * ch + 1:2 * ch + 2], in_=reps[:, 1:2]), sv, 1)
        rep_done = pr.val(sv)

        st_base = pr.val(st)
        for ch in range(KD):
            @blk.sync
            def _(sy, s=s, ch=ch, rep_done=rep_done, st_base=st_base):
                pr.wait(sy, sv)
                pr.inc(sy.dma_start(out=ot[:, :], in_=o_hbm[ch * 128:(ch + 1) * 128, s * tlen:(s + 1) * tlen]), ld, 16)
                pr.inc(sy.dma_start(out=gt[:, :], in_=g_hbm[ch * 128:(ch + 1) * 128, s * tlen:(s + 1) * tlen]), ld, 16)
            ap_tok = pr.val(ld)

            @blk.vector
            def _(v, s=s, ch=ch, ap_tok=ap_tok, rep_done=rep_done, st_base=st_base):
                v.wait_ge(pr.sem(ld), ap_tok)
                v.wait_ge(pr.sem(sv), rep_done)
                if ch >= 2:
                    v.wait_ge(pr.sem(st), st_base + (ch - 1) * 16)
                v.tensor_scalar(out=sq[:, :], in0=ot[:, :], scalar1=repc[:, 2 * ch:2 * ch + 1],
                                scalar2=repc[:, 2 * ch + 1:2 * ch + 2], op0=OP.mult, op1=OP.subtract)
                v.tensor_scalar(out=sq[:, :], in0=sq[:, :], scalar1=gw[:, ch:ch + 1],
                                scalar2=gb[:, ch:ch + 1], op0=OP.mult, op1=OP.add)
                pr.inc(v.tensor_mul(out=ob[ch % 2][:, :], in0=sq[:, :], in1=gt[:, :]), sv, 1)
            app_tok = pr.val(sv)

            @blk.sync
            def _(sy, s=s, ch=ch, app_tok=app_tok):
                sy.wait_ge(pr.sem(sv), app_tok)
                pr.inc(sy.dma_start(out=out_hbm[ch * 128:(ch + 1) * 128, s * tlen:(s + 1) * tlen],
                                    in_=ob[ch % 2][:, :]), st, 16)

    @blk.sync
    def _(sy):
        pr.wait(sy, st)
    pr.set_done([(st, pr.val(st))])
    sc_.close()


def build_l0_attn_b(pr, C, io, x_in, x_out, scr, ntok=NTOK, nseq=B, tlen=T):
    u = scr["u"]
    emit_ln(pr, C, x_in, scr["xhat"], "lb" + u, ntok=ntok)
    emit_gemm(pr, C, "qc" + u, scr["xhat"], io["wq"], scr["qc"], D, D, ntok=ntok,
              epilogue="bias_bf", bias_row_hbm=io["bq"])
    emit_gemm(pr, C, "qp" + u, scr["xhat"], io["wqp"], scr["qp"], D, D, ntok=ntok,
              epilogue="bias_bf", bias_row_hbm=io["bqp"])
    emit_rope(pr, C, scr["qc"], scr["qp"], io["ropecosq"], io["ropesinq"], scr["qr"],
              "rq" + u, nseq=nseq, tlen=tlen)
    emit_gemm(pr, C, "kc" + u, scr["xhat"], io["wk"], scr["qc"], D, D, ntok=ntok,
              epilogue="bias_bf", bias_row_hbm=io["bk"])
    emit_gemm(pr, C, "kp" + u, scr["xhat"], io["wkp"], scr["qp"], D, D, ntok=ntok,
              epilogue="bias_bf", bias_row_hbm=io["bkp"])
    emit_rope(pr, C, scr["qc"], scr["qp"], io["ropecosk"], io["ropesink"], scr["kr"],
              "rk" + u, nseq=nseq, tlen=tlen)
    emit_gemm_tm(pr, C, "vt" + u, scr["xhat"], io["wvx"], io["bvx"], scr["vtm"], D, VW, ntok=ntok)
    emit_gemm(pr, C, "gg" + u, scr["xhat"], io["wg"], scr["g"], D, D, ntok=ntok,
              epilogue="act_bias_bf", bias_hbm=io["bgc"], act=AF.Silu)
    ekldc = emit_pdld(pr, C, scr["xhat"], io["wdT"], io["bdr"], scr["ld_scr"], scr["eq_scr"],
                      "pd" + u, nseq=nseq, tlen=tlen)
    emit_attn_core(pr, C, scr["qr"], scr["kr"], scr["vtm"], scr["eq_scr"], ekldc,
                   scr["o"], "ac" + u, nseq=nseq, tlen=tlen)
    emit_gn_gate(pr, C, scr["o"], scr["g"], io["gnw"], io["gnb"], scr["stat_scr"],
                 scr["og"], "gn" + u, nseq=nseq, tlen=tlen)
    emit_gemm(pr, C, "op" + u, scr["og"], io["wo"], None, D, D, ntok=ntok,
              epilogue="bias_resid_f32", bias_row_hbm=io["bor"],
              resid_hbm=x_in, resout_hbm=x_out)



NE = 8
HID = 4096
NBIS = 34


def moe_cap(tlen):
    return int(tlen * 2.0 / NE) + 1


def moe_slotp(nseq, tlen):
    s = nseq * moe_cap(tlen)
    return (s + 127) // 128 * 128


def moe_consts_host(nseq=B, tlen=T):
    cap = moe_cap(tlen)
    c = {}
    ro = np.zeros((2, 64, 1), np.float32)
    ro[0, 32:, 0] = 1 * cap
    ro[1, :32, 0] = 2 * cap
    ro[1, 32:, 0] = 3 * cap
    c["rowoff0"] = np.ascontiguousarray(ro[0])
    c["rowoff1"] = np.ascontiguousarray(ro[1])
    c["ident_f32"] = np.eye(128, dtype=np.float32)
    return c


def emit_fm2tm(pr, C, src_fm, dst_tm, tag, ntok=NTOK, d_model=D, dt=BF16):
    blk = pr.block
    kd = d_model // 128
    mm = ntok // 128
    sc_ = ExitStack()
    it = [pr.sb(f"{tag}i{i}", [128, 128], dt, sc_) for i in range(2)]
    ot = [pr.sb(f"{tag}o{i}", [128, 128], dt, sc_) for i in range(2)]
    ps = [pr.ps(f"{tag}p{i}", [128, 128], dt, sc_) for i in range(2)]
    st, spe, sev = f"{tag}st", f"{tag}pe", f"{tag}ev"
    tiles = [(m, i) for m in range(mm) for i in range(kd)]
    ldt = []
    spe0, sev0, st0 = pr.val(spe), pr.val(sev), pr.val(st)

    def _store(sy, jdx):
        m, i = tiles[jdx]
        pr.wait(sy, sev, sev0 + jdx + 1)
        pr.inc(sy.dma_start(out=dst_tm[m * 128:(m + 1) * 128, i * 128:(i + 1) * 128],
                            in_=ot[jdx % 2][:, :]), st, 16)

    @blk.sync
    def _(sy):
        pr.entry_wait(sy)
        for idx, (m, i) in enumerate(tiles):
            if idx >= 2:
                _store(sy, idx - 2)
                pr.wait(sy, spe, spe0 + idx - 1)
            pr.inc(sy.dma_start(out=it[idx % 2][:, :],
                                in_=src_fm[i * 128:(i + 1) * 128, m * 128:(m + 1) * 128]),
                   f"{tag}l{idx % 2}", 16)
            ldt.append(pr.val(f"{tag}l{idx % 2}"))
        _store(sy, len(tiles) - 2)
        _store(sy, len(tiles) - 1)

    @blk.tensor
    def _(t):
        C["wait"](t)
        for idx, (m, i) in enumerate(tiles):
            t.wait_ge(pr.sem(f"{tag}l{idx % 2}"), ldt[idx])
            if idx >= 2:
                t.wait_ge(pr.sem(sev), sev0 + idx - 1)
            iden = C["ident_bf"] if dt == BF16 else C["ident_f32"]
            pr.inc(t.transpose(ps[idx % 2][:, :], it[idx % 2][:, :], iden[:, :]), spe, 1)

    @blk.vector
    def _(v):
        for idx, (m, i) in enumerate(tiles):
            v.wait_ge(pr.sem(spe), spe0 + idx + 1)
            if idx >= 2:
                v.wait_ge(pr.sem(st), st0 + (idx - 1) * 16)
            pr.inc(v.tensor_copy(out=ot[idx % 2][:, :], in_=ps[idx % 2][:, :]), sev, 1)

    @blk.sync
    def _(sy):
        pr.wait(sy, st)
    pr.set_done([(st, pr.val(st))])
    sc_.close()


def emit_tm_add_fm(pr, C, src_tm, x_fm, out_fm, tag, ntok=NTOK, d_model=D):
    blk = pr.block
    kd = d_model // 128
    mm = ntok // 128
    sc_ = ExitStack()
    it = [pr.sb(f"{tag}i{i}", [128, 128], F32, sc_) for i in range(2)]
    xt = [pr.sb(f"{tag}x{i}", [128, 128], F32, sc_) for i in range(2)]
    ot = [pr.sb(f"{tag}o{i}", [128, 128], F32, sc_) for i in range(2)]
    ps = [pr.ps(f"{tag}p{i}", [128, 128], F32, sc_) for i in range(2)]
    st, spe, sev = f"{tag}st", f"{tag}pe", f"{tag}ev"
    tiles = [(i, m) for i in range(kd) for m in range(mm)]
    ldt = []
    spe0, sev0, st0 = pr.val(spe), pr.val(sev), pr.val(st)

    def _store(sy, jdx):
        i, m = tiles[jdx]
        pr.wait(sy, sev, sev0 + jdx + 1)
        pr.inc(sy.dma_start(out=out_fm[i * 128:(i + 1) * 128, m * 128:(m + 1) * 128],
                            in_=ot[jdx % 2][:, :]), st, 16)

    @blk.sync
    def _(sy):
        pr.entry_wait(sy)
        for idx, (i, m) in enumerate(tiles):
            if idx >= 2:
                _store(sy, idx - 2)
                pr.wait(sy, sev, sev0 + idx - 1)
            pr.inc(sy.dma_start(out=it[idx % 2][:, :],
                                in_=src_tm[m * 128:(m + 1) * 128, i * 128:(i + 1) * 128]),
                   f"{tag}l{idx % 2}", 16)
            pr.inc(sy.dma_start(out=xt[idx % 2][:, :],
                                in_=x_fm[i * 128:(i + 1) * 128, m * 128:(m + 1) * 128]),
                   f"{tag}l{idx % 2}", 16)
            ldt.append(pr.val(f"{tag}l{idx % 2}"))
        _store(sy, len(tiles) - 2)
        _store(sy, len(tiles) - 1)

    @blk.tensor
    def _(t):
        C["wait"](t)
        for idx, (i, m) in enumerate(tiles):
            t.wait_ge(pr.sem(f"{tag}l{idx % 2}"), ldt[idx])
            if idx >= 2:
                t.wait_ge(pr.sem(sev), sev0 + idx - 1)
            pr.inc(t.transpose(ps[idx % 2][:, :], it[idx % 2][:, :], C["ident_f32"][:, :]), spe, 1)

    @blk.vector
    def _(v):
        for idx, (i, m) in enumerate(tiles):
            v.wait_ge(pr.sem(spe), spe0 + idx + 1)
            v.wait_ge(pr.sem(f"{tag}l{idx % 2}"), ldt[idx])
            if idx >= 2:
                v.wait_ge(pr.sem(st), st0 + (idx - 1) * 16)
            pr.inc(v.tensor_add(out=ot[idx % 2][:, :], in0=ps[idx % 2][:, :], in1=xt[idx % 2][:, :]), sev, 1)

    @blk.sync
    def _(sy):
        pr.wait(sy, st)
    pr.set_done([(st, pr.val(st))])
    sc_.close()


def emit_moe_select(pr, C, xhat_fm, rwT_hbm, idxd_hbm, tag, ntok=NTOK, nseq=B, tlen=T):
    blk = pr.block
    kd = D // 128
    mm = ntok // 128
    mseq = tlen // 128
    idxT = pr.sb(f"{tag}idxT", [128, mm * 64], I32)
    sc_ = ExitStack()
    rw = pr.sb(f"{tag}rw", [128, kd * NE], BF16, sc_)
    xt = [pr.sb(f"{tag}x{i}", [128, 128], BF16, sc_) for i in range(16)]
    gps = [pr.ps(f"{tag}gps{i}", [128, NE], F32, sc_) for i in range(2)]
    gts = [pr.sb(f"{tag}gts{i}", [128, NE], F32, sc_) for i in range(2)]
    grow = pr.sb(f"{tag}grow", [128, 4], F32, sc_)
    tps = [pr.ps(f"{tag}tps{i}", [NE, 128], F32, sc_) for i in range(2)]
    nbt = (nseq + 1) // 2
    bis = [pr.sb(f"{tag}bis{i}", [64, tlen], F32, sc_) for i in range(nbt)]
    th = pr.sb(f"{tag}th", [64, 8], F32, sc_)
    condi = pr.sb(f"{tag}ci", [64, 1], I32, sc_)
    msk = pr.sb(f"{tag}msk", [64, tlen], F32, sc_)
    msk2 = pr.sb(f"{tag}msk2", [64, tlen], F32, sc_)
    rnk = pr.sb(f"{tag}rnk", [64, tlen], F32, sc_)
    zer = pr.sb(f"{tag}zer", [64, tlen], F32, sc_)
    dst = [pr.sb(f"{tag}dst{i}", [64, tlen], F32, sc_) for i in range(nbt)]
    ips = [pr.ps(f"{tag}ips{i}", [128, 64], F32, sc_) for i in range(2)]
    ro = [pr.sb(f"{tag}ro{i}", [64, 1], F32, sc_) for i in range(nbt)]
    ld, st, sv, sa, spe = f"{tag}ld", f"{tag}st", f"{tag}v", f"{tag}a", f"{tag}pe"

    @blk.gpsimd
    def _(g):
        pr.entry_wait(g)
        pr.inc(g.dma_start(out=rw[:, :].rearrange("p (k n) -> p k n", k=kd),
                           in_=rwT_hbm.rearrange("(k p) n -> p k n", p=128)), f"{tag}lg", 16)
    rw_tok = pr.val(f"{tag}lg")

    @blk.sync
    def _(sy):
        pr.entry_wait(sy)
        pr.inc(sy.dma_start(out=ro[0][:, :], in_=pr.moe_rowoff0), ld, 16)
        if nbt > 1:
            pr.inc(sy.dma_start(out=ro[1][:, :], in_=pr.moe_rowoff1), ld, 16)
    ro_tok = pr.val(ld)

    @blk.vector
    def _(v):
        pr.entry_wait(v)
        for bt in range(nbt):
            ii = v.memset(bis[bt][:, :], 0.0)
        pr.inc(ii, sv, 1)

    xl_toks = []
    for m in range(mm):
        @blk.sync
        def _(sy, m=m):
            if m == 0:
                pr.entry_wait(sy)
            if m >= 2:
                pr.wait(sy, spe)
            for i in range(kd):
                pr.inc(sy.dma_start(out=xt[(m % 2) * 8 + i][:, :],
                                    in_=xhat_fm[i * 128:(i + 1) * 128, m * 128:(m + 1) * 128]),
                       f"{tag}l{m % 2}", 16)
        xl_toks.append(pr.val(f"{tag}l{m % 2}"))

        @blk.tensor
        def _(t, m=m):
            if m == 0:
                t.wait_ge(pr.sem(f"{tag}lg"), rw_tok)
                C["wait"](t)
            t.wait_ge(pr.sem(f"{tag}l{m % 2}"), xl_toks[m])
            if m >= 2:
                pr.wait(t, sa)
            for i in range(kd):
                mmx = t.matmul(gps[m % 2][:, :], xt[(m % 2) * 8 + i][:, :], rw[:, i * NE:(i + 1) * NE],
                               start=(i == 0), stop=(i == kd - 1))
            pr.inc(mmx, spe, 1)
        mm_tok = pr.val(spe)

        @blk.scalar
        def _(sc, m=m, mm_tok=mm_tok):
            sc.wait_ge(pr.sem(spe), mm_tok)
            if m >= 2:
                pr.wait(sc, sv)
            pr.inc(sc.activation(out=gts[m % 2][:, :], in_=gps[m % 2][:, :], func=AF.Exp), sa, 1)
        e_tok = pr.val(sa)

        @blk.vector
        def _(v, m=m, e_tok=e_tok):
            v.wait_ge(pr.sem(sa), e_tok)
            pr.selfwait(v, v.reduce_sum(grow[:, 0:1], gts[m % 2][:, :], axis=AX.X))
            pr.selfwait(v, v.reciprocal(grow[:, 1:2], grow[:, 0:1]))
            pr.inc(v.tensor_scalar(out=gts[m % 2][:, :], in0=gts[m % 2][:, :], scalar1=grow[:, 1:2],
                                   scalar2=None, op0=OP.mult), sv, 1)
        n_tok = pr.val(sv)

        @blk.tensor
        def _(t, m=m, n_tok=n_tok):
            t.wait_ge(pr.sem(sv), n_tok)
            if m >= 2:
                pr.wait(t, sv)
            pr.inc(t.transpose(tps[m % 2][:, :], gts[m % 2][:, :], C["ident_f32"][:, :]), spe, 1)
        tp_tok = pr.val(spe)

        s = m // mseq
        btile, brow = s // 2, (s % 2) * 32
        t0 = (m % mseq) * 128

        @blk.vector
        def _(v, m=m, tp_tok=tp_tok, btile=btile, brow=brow, t0=t0):
            v.wait_ge(pr.sem(spe), tp_tok)
            pr.inc(v.tensor_copy(out=bis[btile][brow:brow + NE, t0:t0 + 128], in_=tps[m % 2][:, :]), sv, 1)

    if getattr(pr, "moe_dbg", None) is not None:
        dbg_tok = pr.val(sv)

        @blk.sync
        def _(sy):
            sy.wait_ge(pr.sem(sv), dbg_tok)
            for bt in range(nbt):
                pr.inc(sy.dma_start(out=pr.moe_dbg[bt], in_=bis[bt][:, :]), f"{tag}dbg", 16)

    lo, hi, thc, cnt, cond, need, t1c, t2c = [th[:, i:i + 1] for i in range(8)]
    for bt in range(nbt):
        g = bis[bt]
        ph_base = pr.val(sv)

        @blk.vector
        def _(v, bt=bt, g=g):
            pr.wait(v, sv)
            v.memset(zer[:, :], 0.0)
            pr.selfwait(v, v.memset(lo, 0.0))
            pr.selfwait(v, v.memset(hi, 1.0))
            for it_ in range(NBIS):
                pr.selfwait(v, v.tensor_add(out=thc, in0=lo, in1=hi))
                pr.selfwait(v, v.tensor_scalar_mul(thc, thc, 0.5))
                v.tensor_single_scalar(out=msk[:, :], in_=g[:, :], scalar=thc, op=OP.is_gt)
                pr.selfwait(v, v.reduce_sum(cnt, msk[:, :], axis=AX.X))
                pr.selfwait(v, v.tensor_single_scalar(out=condi[:, :], in_=cnt, scalar=float(moe_cap(tlen)) - 0.5, op=OP.is_gt))
                pr.selfwait(v, v.select(out=t1c, mask=condi[:, :], on_true=thc, on_false=lo))
                pr.selfwait(v, v.select(out=t2c, mask=condi[:, :], on_true=hi, on_false=thc))
                pr.selfwait(v, v.tensor_copy(out=lo, in_=t1c))
                pr.selfwait(v, v.tensor_copy(out=hi, in_=t2c))
            v.tensor_single_scalar(out=msk[:, :], in_=g[:, :], scalar=hi, op=OP.is_gt)
            pr.selfwait(v, v.tensor_single_scalar(out=msk2[:, :], in_=g[:, :], scalar=lo, op=OP.is_gt))
            v.tensor_sub(out=msk2[:, :], in0=msk2[:, :], in1=msk[:, :])
            pr.selfwait(v, v.reduce_sum(cnt, msk[:, :], axis=AX.X))
            pr.selfwait(v, v.tensor_scalar(out=need, in0=cnt, scalar1=-1.0, scalar2=float(moe_cap(tlen)),
                                           op0=OP.mult, op1=OP.add))
            v.tensor_tensor_scan(out=rnk[:, :], data0=zer[:, :], data1=msk2[:, :], initial=0.0, op0=OP.add, op1=OP.add)
            pr.selfwait(v, v.tensor_sub(out=rnk[:, :], in0=rnk[:, :], in1=msk2[:, :]))
            v.tensor_single_scalar(out=rnk[:, :], in_=rnk[:, :], scalar=need, op=OP.is_lt)
            pr.selfwait(v, v.tensor_mul(out=msk2[:, :], in0=msk2[:, :], in1=rnk[:, :]))
            v.tensor_add(out=msk[:, :], in0=msk[:, :], in1=msk2[:, :])
            pr.selfwait(v, v.tensor_tensor_scan(out=rnk[:, :], data0=zer[:, :], data1=msk[:, :], initial=0.0,
                                                op0=OP.add, op1=OP.add))
            v.tensor_sub(out=rnk[:, :], in0=rnk[:, :], in1=msk[:, :])
            pr.selfwait(v, v.tensor_single_scalar(out=rnk[:, :], in_=rnk[:, :], scalar=ro[bt][:, 0:1], op=OP.add))
            v.tensor_scalar(out=msk2[:, :], in0=msk[:, :], scalar1=-1e9, scalar2=1e9, op0=OP.mult, op1=OP.add)
            pr.selfwait(v, v.tensor_mul(out=rnk[:, :], in0=rnk[:, :], in1=msk[:, :]))
            pr.inc(v.tensor_add(out=dst[bt][:, :], in0=rnk[:, :], in1=msk2[:, :]), sv, 1)
        for mc in range(mseq):
            dtok = pr.val(sv)

            @blk.tensor
            def _(t, bt=bt, mc=mc, dtok=dtok):
                t.wait_ge(pr.sem(sv), dtok)
                if mc >= 2:
                    pr.wait(t, sv)
                pr.inc(t.transpose(ips[mc % 2][:, :], dst[bt][:, mc * 128:(mc + 1) * 128],
                                   C["ident_f32"][0:64, 0:64]), spe, 1)
            ttok = pr.val(spe)

            @blk.vector
            def _(v, bt=bt, mc=mc, ttok=ttok):
                v.wait_ge(pr.sem(spe), ttok)
                m_even = (2 * bt) * mseq + mc
                m_odd = (2 * bt + 1) * mseq + mc
                v.tensor_copy(out=idxT[:, m_even * 64:m_even * 64 + 32], in_=ips[mc % 2][:, 0:32])
                pr.inc(v.tensor_copy(out=idxT[:, m_odd * 64 + 32:m_odd * 64 + 64],
                                     in_=ips[mc % 2][:, 32:64]), sv, 1)
            cp_tok = pr.val(sv)

            @blk.sync
            def _(sy, bt=bt, mc=mc, cp_tok=cp_tok):
                sy.wait_ge(pr.sem(sv), cp_tok)
                m_even = (2 * bt) * mseq + mc
                m_odd = (2 * bt + 1) * mseq + mc
                pr.inc(sy.dma_start(out=idxd_hbm[m_even],
                                    in_=idxT[:, m_even * 64:m_even * 64 + NE]), st, 16)
                pr.inc(sy.dma_start(out=idxd_hbm[m_odd],
                                    in_=idxT[:, m_odd * 64 + 32:m_odd * 64 + 32 + NE]), st, 16)

    @blk.vector
    def _(v):
        pr.inc(v.engine_nop(), sv, 1)
    pr.set_done([(sv, pr.val(sv)), (spe, pr.val(spe)), (st, pr.val(st))])
    sc_.close()


def idx_col(m, s, e):
    return m * 64 + (s % 2) * 32 + e


def emit_moe_scatter(pr, C, xtm_hbm, idxd_hbm, staging, tag, ntok=NTOK, nseq=B, tlen=T):
    blk = pr.block
    mm = ntok // 128
    slotp = moe_slotp(nseq, tlen)
    npad = slotp - nseq * moe_cap(tlen)
    sc_ = ExitStack()
    xr = pr.sb(f"{tag}x", [128, D], BF16, sc_)
    idxc = pr.sb(f"{tag}ic", [128, NE], I32, sc_)
    zpad = pr.sb(f"{tag}z", [128, D], BF16, sc_)
    lx, mix = f"{tag}lx", f"{tag}ix"

    @blk.vector
    def _(v):
        pr.entry_wait(v)
        pr.inc(v.memset(zpad[:, :], 0.0), f"{tag}zv", 1)
    z_tok = pr.val(f"{tag}zv")

    @blk.sync
    def _(sy):
        pr.entry_wait(sy)
        sy.wait_ge(pr.sem(f"{tag}zv"), z_tok)
        for e in range(NE):
            pr.inc(sy.dma_start(out=staging[e][slotp - npad:slotp, :], in_=zpad[0:npad, :]),
                   f"{tag}zs", 16)
        with sy.Fori(0, mm) as i:
            sy.wait_ge(pr.sem(mix), i * (NE * 16))
            sy.dma_start(out=xr[:, :], in_=xtm_hbm[bass.ds(i * 128, 128), :]).then_inc(pr.sem(lx), 16)
            sy.dma_start(out=idxc[:, :],
                         in_=idxd_hbm[bass.ds(i, 1)].rearrange("o p e -> p (o e)")).then_inc(pr.sem(lx), 16)

    @blk.gpsimd
    def _(g):
        pr.entry_wait(g)
        with g.Fori(0, mm) as i:
            g.wait_ge(pr.sem(lx), i * 32 + 32)
            for e in range(NE):
                ins = g.indirect_dma_start(
                    out=staging[e], out_offset=IndirectOffsetOnAxis(ap=idxc[:, e:e + 1], axis=0),
                    in_=xr[:, :], in_offset=None,
                    bounds_check=slotp - 1, oob_is_err=False)
                ins.then_inc(pr.sem(mix), 16)

    @blk.gpsimd
    def _(g):
        g.wait_ge(pr.sem(mix), mm * NE * 16)
        g.wait_ge(pr.sem(f"{tag}zs"), pr.val(f"{tag}zs"))
    pr.sem(mix)
    pr._sems[mix][1] = mm * NE * 16
    pr._sems[lx][1] = mm * 32
    pr.set_done([(mix, mm * NE * 16), (f"{tag}zs", pr.val(f"{tag}zs"))])
    sc_.close()

def emit_moe_experts(pr, C, staging, rwT_hbm, w1_hbm, b1_hbm, w2_hbm, b2_hbm,
                     y_scr, tag, nseq=B, tlen=T):
    blk = pr.block
    kd = D // 128
    KH = HID // 128
    slotp = moe_slotp(nseq, tlen)
    CW = 512
    sc_ = ExitStack()
    rw = pr.sb(f"{tag}rw", [128, kd * NE], BF16, sc_)
    xeT = pr.sb(f"{tag}xeT", [128, kd * CW], BF16, sc_)
    w1 = pr.sb(f"{tag}w1", [128, kd * HID], BF16, sc_)
    w2 = pr.sb(f"{tag}w2", [128, KH * D], BF16, sc_)
    b1c = pr.sb(f"{tag}b1", [128, KH], F32, sc_)
    b2r = pr.sb(f"{tag}b2", [1, D], BF16, sc_)
    gcols = pr.sb(f"{tag}gc", [128, slotp // 128], F32, sc_)
    grow = pr.sb(f"{tag}gr", [128, 2], F32, sc_)
    gps = pr.ps(f"{tag}gps", [128, NE], F32, sc_)
    gsb = pr.sb(f"{tag}gsb", [128, NE], F32, sc_)
    hps = [pr.ps(f"{tag}hp{i}", [128, 512], F32, sc_) for i in range(2)]
    hsb = pr.sb(f"{tag}hsb", [128, KH * 512], BF16, sc_)
    yps = pr.ps(f"{tag}yps", [128, 1024], F32, sc_)
    ysb = [pr.sb(f"{tag}ysb{i}", [128, D], F32, sc_) for i in range(2)]
    ld, st, sv, sa, spe, lwg = (f"{tag}ld", f"{tag}st", f"{tag}v", f"{tag}a",
                                f"{tag}pe", f"{tag}lwg")

    @blk.gpsimd
    def _(g):
        pr.entry_wait(g)
        pr.inc(g.dma_start(out=rw[:, :].rearrange("p (k n) -> p k n", k=kd),
                           in_=rwT_hbm.rearrange("(k p) n -> p k n", p=128)), lwg, 16)
    rw_tok = pr.val(lwg)

    chunks = []
    c0 = 0
    while c0 < slotp:
        cw = min(CW, slotp - c0)
        chunks.append((c0, cw))
        c0 += cw

    xeT_free = None
    for e in range(NE):
        @blk.gpsimd
        def _(g, e=e):
            if e > 0:
                pr.wait(g, st)
            pr.inc(g.dma_start(out=w1[:, :], in_=w1_hbm[e]), lwg, 16)
            pr.inc(g.dma_start(out=w2[:, :], in_=w2_hbm[e]), lwg, 16)
            pr.inc(g.dma_start(out=b2r[0:1, :], in_=b2_hbm[e]), lwg, 16)
        wl_tok = pr.val(lwg)

        @blk.sync
        def _(sy, e=e):
            if e == 0:
                pr.entry_wait(sy)
            else:
                pr.wait(sy, st)
            pr.inc(sy.dma_start(out=b1c[:, :], in_=b1_hbm[e]), ld, 16)

        for ci, (c0, cw) in enumerate(chunks):
            @blk.sync
            def _(sy, e=e, c0=c0, cw=cw, xf=xeT_free):
                if xf is not None:
                    sy.wait_ge(pr.sem(spe), xf)
                for i in range(kd):
                    pr.inc(sy.dma_start(out=xeT[:, i * cw:(i + 1) * cw],
                                        in_=staging[e][c0:c0 + cw, i * 128:(i + 1) * 128],
                                        transpose=True), ld, 16)
            xl_tok = pr.val(ld)

            gate_dve = []
            for smi in range(cw // 128):
                sm = (c0 + smi * 128) // 128

                @blk.tensor
                def _(t, e=e, ci=ci, smi=smi, cw=cw, xl_tok=xl_tok):
                    if e == 0 and ci == 0 and smi == 0:
                        t.wait_ge(pr.sem(lwg), rw_tok)
                        C["wait"](t)
                    t.wait_ge(pr.sem(ld), xl_tok)
                    pr.wait(t, sa)
                    for i in range(kd):
                        mmx = t.matmul(gps[:, :], xeT[:, i * cw + smi * 128:i * cw + (smi + 1) * 128],
                                       rw[:, i * NE:(i + 1) * NE], start=(i == 0), stop=(i == kd - 1))
                    pr.inc(mmx, spe, 1)
                g_tok = pr.val(spe)

                @blk.scalar
                def _(sc, g_tok=g_tok, first=(e == 0 and ci == 0 and smi == 0)):
                    sc.wait_ge(pr.sem(spe), g_tok)
                    if not first:
                        pr.wait(sc, sv)
                    pr.inc(sc.activation(out=gsb[:, :], in_=gps[:, :], func=AF.Exp), sa, 1)
                ge_tok = pr.val(sa)

                @blk.vector
                def _(v, e=e, sm=sm, ge_tok=ge_tok):
                    v.wait_ge(pr.sem(sa), ge_tok)
                    pr.selfwait(v, v.reduce_sum(grow[:, 0:1], gsb[:, :], axis=AX.X))
                    pr.selfwait(v, v.reciprocal(grow[:, 1:2], grow[:, 0:1]))
                    pr.inc(v.tensor_tensor(out=gcols[:, sm:sm + 1], in0=gsb[:, e:e + 1],
                                           in1=grow[:, 1:2], op=OP.mult), sv, 1)
                gate_dve.append(pr.val(sv))
            gates_done = pr.val(sv)

            sa_base = pr.val(sa)
            prev_y_done = pr.val(spe)
            h_base = pr.val(spe)

            @blk.tensor
            def _(t, e=e, ci=ci, cw=cw, sa_base=sa_base, wl_tok=wl_tok):
                if ci == 0:
                    t.wait_ge(pr.sem(lwg), wl_tok)
                for hb in range(KH):
                    if hb >= 2:
                        t.wait_ge(pr.sem(sa), sa_base + hb - 1)
                    for i in range(kd):
                        mmx = t.matmul(hps[hb % 2][:, 0:cw],
                                       w1[:, i * HID + hb * 128:i * HID + (hb + 1) * 128],
                                       xeT[:, i * cw:i * cw + cw],
                                       start=(i == 0), stop=(i == kd - 1))
                    pr.inc(mmx, spe, 1)
            xeT_free = pr.val(spe)

            @blk.scalar
            def _(sc, cw=cw, h_base=h_base, prev_y_done=prev_y_done, first=(e == 0 and ci == 0)):
                if not first:
                    sc.wait_ge(pr.sem(spe), prev_y_done)
                for hb in range(KH):
                    sc.wait_ge(pr.sem(spe), h_base + hb + 1)
                    pr.inc(sc.activation(out=hsb[:, hb * 512:hb * 512 + cw], in_=hps[hb % 2][:, 0:cw],
                                         func=AF.Gelu, bias=b1c[:, hb:hb + 1]), sa, 1)
            gelu_done = pr.val(sa)

            for smi in range(cw // 128):
                sm = (c0 + smi * 128) // 128

                @blk.tensor
                def _(t, smi=smi, gelu_done=gelu_done):
                    t.wait_ge(pr.sem(sa), gelu_done)
                    pr.wait(t, sv)
                    for n in range(2):
                        for hb in range(KH):
                            t.matmul(yps[:, n * 512:(n + 1) * 512],
                                     hsb[:, hb * 512 + smi * 128:hb * 512 + (smi + 1) * 128],
                                     w2[:, hb * D + n * 512:hb * D + (n + 1) * 512],
                                     start=(hb == 0), stop=False)
                        mmx = t.matmul(yps[:, n * 512:(n + 1) * 512], C["ones_row_bf"][0:1, :],
                                       b2r[0:1, n * 512:(n + 1) * 512], start=False, stop=True)
                    pr.inc(mmx, spe, 1)
                yt = pr.val(spe)

                @blk.vector
                def _(v, sm=sm, yt=yt, gates_done=gates_done):
                    v.wait_ge(pr.sem(spe), yt)
                    v.wait_ge(pr.sem(sv), gates_done)
                    pr.wait(v, st)
                    pr.inc(v.tensor_scalar(out=ysb[sm % 2][:, :], in0=yps[:, :], scalar1=gcols[:, sm:sm + 1],
                                           scalar2=None, op0=OP.mult), sv, 1)
                ye_tok = pr.val(sv)

                @blk.sync
                def _(sy, e=e, sm=sm, ye_tok=ye_tok):
                    sy.wait_ge(pr.sem(sv), ye_tok)
                    pr.inc(sy.dma_start(out=y_scr[e][sm * 128:(sm + 1) * 128, :], in_=ysb[sm % 2][:, :]), st, 16)

    @blk.sync
    def _(sy):
        pr.wait(sy, st)
    pr.set_done([(st, pr.val(st))])
    sc_.close()


def emit_moe_writeback(pr, C, y_scr, idxd_hbm, moe_tm, tag, ntok=NTOK, nseq=B, tlen=T):
    blk = pr.block
    mm = ntok // 128
    slotp = moe_slotp(nseq, tlen)
    sc_ = ExitStack()
    yt = [pr.sb(f"{tag}y{e}", [128, D], F32, sc_) for e in range(NE)]
    acc = pr.sb(f"{tag}acc", [128, D], F32, sc_)
    idxc = pr.sb(f"{tag}ic", [128, NE], I32, sc_)
    idxf = pr.sb(f"{tag}if", [128, NE], F32, sc_)
    msk8 = pr.sb(f"{tag}mk", [128, NE], F32, sc_)
    lx, gix, cv, st = f"{tag}lx", f"{tag}ix", f"{tag}cv", f"{tag}st"

    @blk.vector
    def _(v):
        pr.entry_wait(v)
        for e in range(NE):
            ii = v.memset(yt[e][:, :], 0.0)
        pr.inc(ii, f"{tag}zv", 1)
    z_tok = pr.val(f"{tag}zv")

    @blk.sync
    def _(sy):
        pr.entry_wait(sy)
        with sy.Fori(0, mm) as i:
            sy.wait_ge(pr.sem(gix), i * (NE * 16))
            sy.dma_start(out=idxc[:, :],
                         in_=idxd_hbm[bass.ds(i, 1)].rearrange("o p e -> p (o e)")).then_inc(pr.sem(lx), 16)
            sy.wait_ge(pr.sem(cv), i * 3 + 3)
            sy.dma_start(out=moe_tm[bass.ds(i * 128, 128), :], in_=acc[:, :]).then_inc(pr.sem(st), 16)

    @blk.gpsimd
    def _(g):
        pr.entry_wait(g)
        g.wait_ge(pr.sem(f"{tag}zv"), z_tok)
        with g.Fori(0, mm) as i:
            g.wait_ge(pr.sem(lx), i * 16 + 16)
            g.wait_ge(pr.sem(cv), i * 3)
            for e in range(NE):
                ins = g.indirect_dma_start(
                    out=yt[e][:, :], out_offset=None,
                    in_=y_scr[e], in_offset=IndirectOffsetOnAxis(ap=idxc[:, e:e + 1], axis=0),
                    bounds_check=slotp - 1, oob_is_err=False)
                ins.then_inc(pr.sem(gix), 16)

    @blk.vector
    def _(v):
        with v.Fori(0, mm) as i:
            v.wait_ge(pr.sem(gix), i * (NE * 16) + NE * 16)
            v.wait_ge(pr.sem(st), i * 16)
            v.tensor_copy(out=idxf[:, :], in_=idxc[:, :]).then_inc(pr.sem(cv), 1)
            v.wait_ge(pr.sem(cv), i * 3 + 1)
            v.tensor_single_scalar(out=msk8[:, :], in_=idxf[:, :], scalar=5e8,
                                   op=OP.is_lt).then_inc(pr.sem(cv), 1)
            v.wait_ge(pr.sem(cv), i * 3 + 2)
            v.memset(acc[:, :], 0.0)
            for e in range(NE):
                ii = v.scalar_tensor_tensor(out=acc[:, :], in0=yt[e][:, :],
                                            scalar=msk8[:, e:e + 1], in1=acc[:, :],
                                            op0=OP.mult, op1=OP.add)
            ii.then_inc(pr.sem(cv), 1)

    @blk.sync
    def _(sy):
        sy.wait_ge(pr.sem(st), mm * 16)
    for nm, tot in ((lx, mm * 16), (gix, mm * NE * 16), (cv, mm * 3), (st, mm * 16)):
        pr.sem(nm)
        pr._sems[nm][1] = tot
    pr.set_done([(st, mm * 16)])
    sc_.close()

def build_l0_ffn(pr, C, io, x_in, x_out, scr, ntok=NTOK, nseq=B, tlen=T):
    u = scr["u"]
    emit_ln(pr, C, x_in, scr["xhat"], "lm" + u, ntok=ntok)
    emit_fm2tm(pr, C, scr["xhat"], scr["xtm"], "tt" + u, ntok=ntok)
    emit_moe_select(pr, C, scr["xhat"], io["rwT"], scr["idxd"], "ms" + u, ntok=ntok, nseq=nseq, tlen=tlen)
    emit_moe_scatter(pr, C, scr["xtm"], scr["idxd"], scr["staging"], "mc" + u, ntok=ntok, nseq=nseq, tlen=tlen)
    emit_moe_experts(pr, C, scr["staging"], io["rwT"], io["w1h"], io["b1c"], io["w2n"], io["b2r"],
                     scr["y_scr"], "me" + u, nseq=nseq, tlen=tlen)
    emit_moe_writeback(pr, C, scr["y_scr"], scr["idxd"], scr["moe_tm"], "mw" + u, ntok=ntok, nseq=nseq, tlen=tlen)
    emit_tm_add_fm(pr, C, scr["moe_tm"], x_in, x_out, "ma" + u, ntok=ntok)



class _SkipBlk:
    def __getattr__(self, name):
        return lambda f: None


_skipblk = _SkipBlk()

NH6 = 16
HQK = 32
KEYD = 512
HV = 64
LCH = 64


def rwkv_consts_host():
    c = {}
    tri = np.triu(np.ones((64, 64), np.float32), 1)
    c["masksel_i32"] = np.ascontiguousarray(np.tile(tri, (1, 16)).astype(np.int32))
    hsp = np.zeros((64, 2), bf16)
    hsp[0:32, 0] = 1
    hsp[32:64, 1] = 1
    c["hselp2_bf"] = hsp
    hs2 = np.zeros((128, 2), bf16)
    hs2[0:64, 0] = 1
    hs2[64:128, 1] = 1
    c["hsel2_bf"] = hs2
    rp2 = np.zeros((2, 128), bf16)
    rp2[0, 0:64] = 1
    rp2[1, 64:128] = 1
    c["repl2_bf"] = rp2
    c["zeros64"] = np.zeros((128, 64), np.float32)
    c["zs1k"] = np.zeros((64, 1024), np.float32)
    c["zcol"] = np.zeros((128, 1), np.float32)
    return c


def rwkv_host_io(W_x1, W_x2, x_bias, mu_x, W_r, W_w1, W_w2, b_w2, W_k, W_v,
                 W_g, u, gn_w, gn_b, W_o):
    io = {}
    wx1p = np.zeros((D, 256), np.float32)
    wx1p[:, 0:160] = W_x1
    io["wx1"] = lhsT_tiles(wx1p)
    io["wx2"] = np.ascontiguousarray(
        np.concatenate([np.ascontiguousarray(W_x2[:, i, :].T) for i in range(5)], axis=1))
    io["xb5"] = np.stack([bias_cols(x_bias[i]) for i in range(5)])
    io["mux"] = bias_cols(mu_x)
    io["wr"] = lhsT_tiles(W_r)
    ww1p = np.zeros((D, 128), np.float32)
    ww1p[:, 0:64] = W_w1
    io["ww1"] = lhsT_tiles(ww1p)
    ww2p = np.zeros((128, KEYD), np.float32)
    ww2p[0:64, :] = W_w2
    io["ww2"] = lhsT_tiles(ww2p)
    io["bw2c"] = bias_cols(b_w2)
    io["wk"] = lhsT_tiles(W_k)
    io["wv"] = np.ascontiguousarray(W_v)
    io["wvl"] = lhsT_tiles(W_v)
    io["wg"] = lhsT_tiles(W_g)
    io["wo"] = lhsT_tiles(W_o)
    io["ucol"] = np.ascontiguousarray(u.reshape(-1).reshape(8, 64).T)
    io["gnwx"] = np.ascontiguousarray(np.repeat(gn_w.reshape(8, 128).T, 64, axis=1))
    io["gnbx"] = np.ascontiguousarray(np.repeat(gn_b.reshape(8, 128).T, 64, axis=1))
    io["zc8"] = np.zeros((128, 8), np.float32)
    io["zc2"] = np.zeros((128, 2), np.float32)
    io["zc1"] = np.zeros((128, 1), np.float32)
    io["zrow"] = np.zeros((1, 1024), np.float32)
    return io


def rwkv_io_dram(pr):
    shp = {"wx1": [128, 8 * 256], "wx2": [32, 5 * D], "xb5": [5, 128, 8],
           "mux": [128, 8], "wr": [128, 8 * KEYD], "ww1": [128, 8 * 128],
           "ww2": [128, KEYD], "bw2c": [128, 4], "wk": [128, 8 * KEYD],
           "wv": [D, D], "wvl": [128, 8 * D], "wg": [128, 8 * D], "wo": [128, 8 * D],
           "ucol": [64, 8], "gnwx": [128, 512], "gnbx": [128, 512],
           "zc8": [128, 8], "zc2": [128, 2], "zc1": [128, 1], "zrow": [1, D]}
    return {k: pr.dram("a_" + k, v, F32, kind="ExternalInput") for k, v in shp.items()}


def emit_shift_mix(pr, C, xhat_hbm, mux_hbm, dl_hbm, xm_hbm, tag, ntok=NTOK,
                   nseq=B, tlen=T):
    blk = pr.block
    sc_ = ExitStack()
    xh = [pr.sb(f"{tag}x{i}", [128, tlen], BF16, sc_) for i in range(2)]
    dl = [pr.sb(f"{tag}d{i}", [128, tlen], BF16, sc_) for i in range(2)]
    xm = [pr.sb(f"{tag}m{i}", [128, tlen], BF16, sc_) for i in range(2)]
    mux = pr.sb(f"{tag}mu", [128, 8], F32, sc_)
    ld, st, sv = f"{tag}ld", f"{tag}st", f"{tag}v"

    @blk.sync
    def _(sy):
        pr.entry_wait(sy)
        pr.inc(sy.dma_start(out=mux[:, :], in_=mux_hbm), f"{tag}lm", 16)
    mu_tok = pr.val(f"{tag}lm")

    tiles = [(i, s) for i in range(KD) for s in range(nseq)]
    for idx, (i, s) in enumerate(tiles):
        c0 = s * tlen

        @blk.sync
        def _(sy, idx=idx, i=i, c0=c0):
            if idx == 0:
                pr.entry_wait(sy)
            if idx >= 2:
                pr.wait(sy, st)
            pr.inc(sy.dma_start(out=xh[idx % 2][:, :], in_=xhat_hbm[i * 128:(i + 1) * 128, c0:c0 + tlen]),
                   ld, 16)
        ld_tok = pr.val(ld)

        @blk.vector
        def _(v, idx=idx, i=i, ld_tok=ld_tok):
            if idx == 0:
                v.wait_ge(pr.sem(f"{tag}lm"), mu_tok)
            v.wait_ge(pr.sem(ld), ld_tok)
            a, b_, m_ = xh[idx % 2], dl[idx % 2], xm[idx % 2]
            v.tensor_sub(out=b_[:, 1:tlen], in0=a[:, 0:tlen - 1], in1=a[:, 1:tlen])
            pr.selfwait(v, v.tensor_scalar_mul(b_[:, 0:1], a[:, 0:1], -1.0))
            pr.inc(v.scalar_tensor_tensor(out=m_[:, :], in0=b_[:, :], scalar=mux[:, i:i + 1],
                                          in1=a[:, :], op0=OP.mult, op1=OP.add), sv, 1)
        v_tok = pr.val(sv)

        @blk.sync
        def _(sy, idx=idx, i=i, c0=c0, v_tok=v_tok):
            sy.wait_ge(pr.sem(sv), v_tok)
            pr.inc(sy.dma_start(out=dl_hbm[i * 128:(i + 1) * 128, c0:c0 + tlen], in_=dl[idx % 2][:, :]), st, 16)
            pr.inc(sy.dma_start(out=xm_hbm[i * 128:(i + 1) * 128, c0:c0 + tlen], in_=xm[idx % 2][:, :]), st, 16)

    @blk.sync
    def _(sy):
        pr.wait(sy, st)
    pr.set_done([(st, pr.val(st))])
    sc_.close()


def emit_lora5(pr, C, lowT_hbm, dl_hbm, xhat_hbm, wx2_hbm, xb5_hbm, xi_hbms,
               tag, ntok=NTOK):
    blk = pr.block
    nch = 512
    sc_ = ExitStack()
    wx2 = pr.sb(f"{tag}w", [32, 5 * D], BF16, sc_)
    xb = pr.sb(f"{tag}b", [128, 40], F32, sc_)
    low = [pr.sb(f"{tag}lo{i}", [32, nch], BF16, sc_) for i in range(5)]
    dl = [pr.sb(f"{tag}dl{i}", [128, nch], BF16, sc_) for i in range(8)]
    xh = [pr.sb(f"{tag}xh{i}", [128, nch], BF16, sc_) for i in range(8)]
    ps = [pr.ps(f"{tag}ps{i}", [128, nch], F32, sc_) for i in range(2)]
    e1 = [pr.sb(f"{tag}e{i}", [128, nch], BF16, sc_) for i in range(2)]
    t2 = pr.sb(f"{tag}t2", [128, nch], F32, sc_)
    ob = [pr.sb(f"{tag}o{i}", [128, nch], BF16, sc_) for i in range(2)]
    ld, st, sv, sa, spe = f"{tag}ld", f"{tag}st", f"{tag}v", f"{tag}a", f"{tag}pe"

    @blk.gpsimd
    def _(g):
        pr.entry_wait(g)
        pr.inc(g.dma_start(out=wx2[:, :], in_=wx2_hbm), f"{tag}lw", 16)
    lw_tok = pr.val(f"{tag}lw")

    @blk.sync
    def _(sy):
        pr.entry_wait(sy)
        pr.inc(sy.dma_start(out=xb[:, :].rearrange("p (i c) -> p i c", i=5),
                            in_=xb5_hbm.rearrange("i p c -> p i c")), f"{tag}lb", 16)
    lb_tok = pr.val(f"{tag}lb")

    nnch = ntok // nch
    for n in range(nnch):
        n0 = n * nch
        pe_base, sv_base, sa_base, st_base = pr.val(spe), pr.val(sv), pr.val(sa), pr.val(st)

        @blk.sync
        def _(sy, n=n, n0=n0, sv_base=sv_base):
            if n == 0:
                pr.entry_wait(sy)
            if n > 0:
                pr.wait(sy, sv, sv_base)
            for tdx in range(5):
                pr.inc(sy.dma_start(out=low[tdx][:, :],
                                    in_=lowT_hbm[tdx * 32:(tdx + 1) * 32, n0:n0 + nch]), ld, 16)
            for m in range(KD):
                pr.inc(sy.dma_start(out=dl[m][:, :], in_=dl_hbm[m * 128:(m + 1) * 128, n0:n0 + nch]), ld, 16)
                pr.inc(sy.dma_start(out=xh[m][:, :], in_=xhat_hbm[m * 128:(m + 1) * 128, n0:n0 + nch]), ld, 16)
        ld_tok = pr.val(ld)

        qs = [(i, m) for i in range(5) for m in range(KD)]

        @blk.tensor
        def _(t, n=n, ld_tok=ld_tok, sa_base=sa_base):
            if n == 0:
                t.wait_ge(pr.sem(f"{tag}lw"), lw_tok)
                C["wait"](t)
            t.wait_ge(pr.sem(ld), ld_tok)
            for q, (i, m) in enumerate(qs):
                if q >= 2 or n > 0:
                    t.wait_ge(pr.sem(sa), sa_base + q - 1)
                pr.inc(t.matmul(ps[q % 2][:, :], wx2[0:32, i * D + m * 128:i * D + (m + 1) * 128],
                                low[i][:, :], start=True, stop=True), spe, 1)

        @blk.scalar
        def _(scl, n=n, pe_base=pe_base, sv_base=sv_base):
            if n == 0:
                scl.wait_ge(pr.sem(f"{tag}lb"), lb_tok)
            for q, (i, m) in enumerate(qs):
                scl.wait_ge(pr.sem(spe), pe_base + q + 1)
                if q >= 2:
                    scl.wait_ge(pr.sem(sv), sv_base + 2 * (q - 1))
                pr.inc(scl.activation(out=e1[q % 2][:, :], in_=ps[q % 2][:, :], func=AF.Identity,
                                      bias=xb[:, 8 * i + m:8 * i + m + 1]), sa, 1)

        @blk.vector
        def _(v, n=n, sa_base=sa_base, st_base=st_base):
            for q, (i, m) in enumerate(qs):
                v.wait_ge(pr.sem(sa), sa_base + q + 1)
                if q >= 2:
                    v.wait_ge(pr.sem(st), st_base + (q - 1) * 16)
                v.tensor_mul(out=t2[:, :], in0=e1[q % 2][:, :], in1=dl[m][:, :])
                pr.inc(v.tensor_add(out=ob[q % 2][:, :], in0=t2[:, :], in1=xh[m][:, :]), sv, 2)

        @blk.sync
        def _(sy, n0=n0, sv_base=sv_base):
            for q, (i, m) in enumerate(qs):
                sy.wait_ge(pr.sem(sv), sv_base + 2 * (q + 1))
                pr.inc(sy.dma_start(out=xi_hbms[i][m * 128:(m + 1) * 128, n0:n0 + nch],
                                    in_=ob[q % 2][:, :]), st, 16)

    @blk.sync
    def _(sy):
        pr.wait(sy, st)
    pr.set_done([(st, pr.val(st))])
    sc_.close()


def emit_rwkv6_core(pr, C, io, eT_hbm, rT_hbm, kT_hbm, vtm_hbm, vfm_hbm,
                    gsl_hbm, o_hbm, tag, nseq=B, tlen=T):
    cstage = 5
    c2sub = "d"
    blk = pr.block
    nch_ = tlen // LCH
    sc_ = ExitStack()
    eg = [pr.sb(f"{tag}eg{i}", [64, 512], F32, sc_) for i in range(2)]
    rg = [pr.sb(f"{tag}rg{i}", [64, 512], F32, sc_) for i in range(2)]
    kg = [pr.sb(f"{tag}kg{i}", [64, 512], F32, sc_) for i in range(2)]
    vt = [pr.sb(f"{tag}vt{i}", [64, 1024], BF16, sc_) for i in range(2)]
    vfg = [pr.sb(f"{tag}vf{i}", [128, 512], BF16, sc_) for i in range(2)]
    gsl = [pr.sb(f"{tag}gs{i}", [128, 512], BF16, sc_) for i in range(2)]
    A4 = pr.sb(f"{tag}A", [64, 512], F32, sc_)
    t1g = pr.sb(f"{tag}t1", [64, 512], F32, sc_)
    a1g = pr.sb(f"{tag}a1", [64, 512], F32, sc_)
    a2g = pr.sb(f"{tag}a2", [64, 512], F32, sc_)
    a4g = pr.sb(f"{tag}a4", [64, 512], F32, sc_)
    E1g = pr.sb(f"{tag}E1", [64, 512], F32, sc_)
    E2g = pr.sb(f"{tag}E2", [64, 512], F32, sc_)
    E3g = pr.sb(f"{tag}E3", [64, 512], F32, sc_)
    E4g = pr.sb(f"{tag}E4", [64, 512], F32, sc_)
    rtg = pr.sb(f"{tag}rt", [64, 512], BF16, sc_)
    ktg = pr.sb(f"{tag}kt", [64, 512], BF16, sc_)
    rhg = pr.sb(f"{tag}rh", [64, 512], BF16, sc_)
    khg = pr.sb(f"{tag}kh", [64, 512], BF16, sc_)
    ruk = pr.sb(f"{tag}ru", [64, 512], F32, sc_)
    rukb = pr.sb(f"{tag}rb", [64, 512], BF16, sc_)
    ealg = pr.sb(f"{tag}ea", [64, 8], F32, sc_)
    mrowb = pr.sb(f"{tag}mr", [2, 512], BF16, sc_)
    scm = pr.sb(f"{tag}sc", [64, 1024], BF16, sc_)
    tvd = pr.sb(f"{tag}tv", [128, 512], F32, sc_)
    ot = pr.sb(f"{tag}ot", [128, 512], BF16, sc_)
    sq = pr.sb(f"{tag}sq", [128, 512], BF16, sc_)
    nsb = pr.sb(f"{tag}ns", [2, 2048], F32, sc_)
    lnb = pr.sb(f"{tag}ln", [2, 512], F32, sc_)
    rsf = pr.sb(f"{tag}rf", [2, 512], F32, sc_)
    rsb = pr.sb(f"{tag}rs", [2, 512], BF16, sc_)
    msb = pr.sb(f"{tag}ms", [2, 512], BF16, sc_)
    fbuf = pr.sb(f"{tag}fb", [128, 512], F32, sc_)
    obf = [pr.sb(f"{tag}ob{i}", [128, 512], BF16, sc_) for i in range(2)]
    khT = pr.sb(f"{tag}kT", [64, 512], BF16, sc_)
    s0f = [[pr.sb(f"{tag}s0f{i}{c}", [64, 64], F32, sc_) for c in range(8)] for i in range(2)]
    s0b = [pr.sb(f"{tag}s0b{c}", [64, 64], BF16, sc_) for c in range(8)]
    gnwx = pr.sb(f"{tag}gw", [128, 512], F32, sc_)
    gnbx = pr.sb(f"{tag}gb", [128, 512], F32, sc_)
    ucol = pr.sb(f"{tag}uc", [64, 8], F32, sc_)
    scg = pr.ps(f"{tag}scg", [64, 1024], F32, sc_)
    oPS = pr.ps(f"{tag}oP", [128, 512], F32, sc_)
    stats = pr.ps(f"{tag}sp", [128, 512], F32, sc_)
    ktps = pr.ps(f"{tag}ktp", [64, 512], BF16, sc_)
    repP1 = pr.ps(f"{tag}r1", [128, 512], F32, sc_)
    repP2 = pr.ps(f"{tag}r2", [128, 512], F32, sc_)
    Sps = pr.ps(f"{tag}Sp", [64, 512], F32, sc_)
    ld, st, sv, sa, spe = f"{tag}ld", f"{tag}st", f"{tag}v", f"{tag}a", f"{tag}pe"

    @blk.sync
    def _(sy):
        pr.entry_wait(sy)
        pr.inc(sy.dma_start(out=gnwx[:, :], in_=io["gnwx"]), f"{tag}lg", 16)
        pr.inc(sy.dma_start(out=gnbx[:, :], in_=io["gnbx"]), f"{tag}lg", 16)
        pr.inc(sy.dma_start(out=ucol[:, :], in_=io["ucol"]), f"{tag}lg", 16)
    lg_tok = pr.val(f"{tag}lg")

    prev = None
    prev2 = None
    for b in range(nseq):
        sv_now = pr.val(sv)

        @blk.vector
        def _(v, sv_now=sv_now, b=b):
            if b == 0:
                pr.entry_wait(v)
            v.wait_ge(pr.sem(sv), sv_now)
            for c in range(8):
                ii = v.memset(s0f[1][c][:, :], 0.0)
            pr.inc(ii, sv, 1)

        for ci in range(nch_):
            par = ci % 2
            t0 = b * tlen + ci * LCH
            tk = {}
            B2 = blk if cstage >= 2 else _skipblk
            B3 = blk if cstage >= 3 else _skipblk
            B4 = blk if cstage >= 4 else _skipblk
            B5 = blk if cstage >= 5 else _skipblk

            @blk.sync
            def _(sy, b=b, ci=ci, par=par, t0=t0, prev2=prev2):
                if b == 0 and ci == 0:
                    pr.entry_wait(sy)
                if prev2 is not None:
                    sy.wait_ge(pr.sem(sv), prev2["sv_end"])
                    sy.wait_ge(pr.sem(spe), prev2["spe_end"])
                for c in range(8):
                    pr.inc(sy.dma_start(out=eg[par][:, c * 64:(c + 1) * 64],
                                        in_=eT_hbm[c * 64:(c + 1) * 64, t0:t0 + 64]), ld, 16)
                    pr.inc(sy.dma_start(out=rg[par][:, c * 64:(c + 1) * 64],
                                        in_=rT_hbm[c * 64:(c + 1) * 64, t0:t0 + 64]), ld, 16)
                    pr.inc(sy.dma_start(out=kg[par][:, c * 64:(c + 1) * 64],
                                        in_=kT_hbm[c * 64:(c + 1) * 64, t0:t0 + 64]), ld, 16)
                pr.inc(sy.dma_start(out=vt[par][:, :], in_=vtm_hbm[t0:t0 + 64, :]), ld, 16)
                for j in range(8):
                    pr.inc(sy.dma_start(out=vfg[par][:, j * 64:(j + 1) * 64],
                                        in_=vfm_hbm[j * 128:(j + 1) * 128, t0:t0 + 64]), ld, 16)
                    pr.inc(sy.dma_start(out=gsl[par][:, j * 64:(j + 1) * 64],
                                        in_=gsl_hbm[j * 128:(j + 1) * 128, t0:t0 + 64]), ld, 16)
            tk["ld"] = pr.val(ld)

            @blk.vector
            def _(v, par=par, ldk=tk["ld"]):
                v.wait_ge(pr.sem(ld), ldk)
                for c in range(8):
                    s = slice(c * 64, (c + 1) * 64)
                    ii = v.tensor_tensor_scan(out=A4[:, s], data0=eg[par][:, s],
                                              data1=C["zeros64"][0:64, :], initial=0.0,
                                              op0=OP.add, op1=OP.add)
                pr.selfwait(v, ii)
                pr.selfwait(v, v.tensor_sub(out=t1g[:, :], in0=eg[par][:, :], in1=A4[:, :]))
                for c in range(8):
                    s = slice(c * 64, (c + 1) * 64)
                    amid = A4[:, c * 64 + 31:c * 64 + 32]
                    aL = A4[:, c * 64 + 63:c * 64 + 64]
                    v.tensor_scalar(out=a1g[:, s], in0=t1g[:, s], scalar1=amid, scalar2=None, op0=OP.add)
                    v.tensor_scalar(out=a2g[:, s], in0=A4[:, s], scalar1=amid, scalar2=None, op0=OP.subtract)
                    ii = v.tensor_scalar(out=a4g[:, s], in0=A4[:, s], scalar1=aL, scalar2=None, op0=OP.subtract)
                pr.inc(ii, sv, 1)
            tk["sv_p1"] = pr.val(sv)

            @blk.scalar
            def _(scl, b=b, ci=ci, p1=tk["sv_p1"]):
                if b == 0 and ci == 0:
                    C["wait"](scl)
                scl.wait_ge(pr.sem(sv), p1)
                scl.activation(out=E1g[:, :], in_=a1g[:, :], func=AF.Exp)
                scl.activation(out=E2g[:, :], in_=a2g[:, :], func=AF.Exp)
                scl.activation(out=E3g[:, :], in_=t1g[:, :], func=AF.Exp)
                scl.activation(out=E4g[:, :], in_=a4g[:, :], func=AF.Exp)
                for c in range(8):
                    ii = scl.activation(out=ealg[:, c:c + 1], in_=A4[:, c * 64 + 63:c * 64 + 64],
                                        func=AF.Exp, scale=-1.0, bias=C["zcol"][0:64, :])
                pr.inc(ii, sa, 1)
            tk["sa_ex"] = pr.val(sa)

            @blk.vector
            def _(v, par=par, sax=tk["sa_ex"]):
                v.wait_ge(pr.sem(sa), sax)
                v.tensor_mul(out=rtg[:, :], in0=rg[par][:, :], in1=E1g[:, :])
                v.tensor_mul(out=ktg[:, :], in0=kg[par][:, :], in1=E2g[:, :])
                v.tensor_mul(out=rhg[:, :], in0=rg[par][:, :], in1=E3g[:, :])
                v.tensor_mul(out=khg[:, :], in0=kg[par][:, :], in1=E4g[:, :])
                pr.selfwait(v, v.tensor_mul(out=ruk[:, :], in0=rg[par][:, :], in1=kg[par][:, :]))
                for c in range(8):
                    s = slice(c * 64, (c + 1) * 64)
                    ii = v.tensor_scalar(out=rukb[:, s], in0=ruk[:, s], scalar1=ucol[:, c:c + 1],
                                         scalar2=None, op0=OP.mult)
                pr.inc(ii, sv, 1)
            tk["sv_p2"] = pr.val(sv)

            @B2.tensor
            def _(t, b=b, ci=ci, p2=tk["sv_p2"], prev=prev, lgk=lg_tok):
                if b == 0 and ci == 0:
                    C["wait"](t)
                    t.wait_ge(pr.sem(f"{tag}lg"), lgk)
                t.wait_ge(pr.sem(sv), p2)
                if prev is not None:
                    t.wait_ge(pr.sem(sv), prev["sv_mrow"])
                    t.wait_ge(pr.sem(sv), prev["sv_scm"])
                for c in range(8):
                    mmx = t.matmul(stats[64:66, 64 * c:64 * c + 64], C["hselp2_bf"][0:64, :],
                                   rukb[:, 64 * c:64 * c + 64], start=True, stop=True)
                if c2sub >= "b":
                    for jj in range(2):
                        t.nop()
                        for c in range(8):
                            h = 2 * c + jj
                            mmx = t.matmul(scg[0:64, 64 * h:64 * h + 64],
                                           ktg[32 * jj:32 * jj + 32, 64 * c:64 * c + 64],
                                           rtg[32 * jj:32 * jj + 32, 64 * c:64 * c + 64],
                                           start=True, stop=True)
                pr.inc(mmx, spe, 1)
            tk["spe_sct"] = None if cstage < 2 else pr.val(spe)

            @B2.vector
            def _(v, sct=tk.get("spe_sct")):
                v.wait_ge(pr.sem(spe), sct)
                if c2sub >= "c":
                    ii = v.tensor_copy(out=mrowb[:, :], in_=stats[64:66, :])
                if c2sub >= "d":
                    ii = v.select(out=scm[:, :], mask=C["masksel_i32"][:, :], on_true=scg[:, :],
                                  on_false=C["zs1k"][:, :])
                if c2sub < "c":
                    ii = v.engine_nop()
                pr.inc(ii, sv, 2)
            tk["sv_mrow"] = None if cstage < 2 else pr.val(sv)
            tk["sv_scm"] = tk["sv_mrow"]

            @B3.tensor
            def _(t, ci=ci, par=par, mk=tk["sv_mrow"], prev=prev):
                t.wait_ge(pr.sem(sv), mk)
                if prev is not None:
                    t.wait_ge(pr.sem(sv), prev["sv_fin"])
                    t.wait_ge(pr.sem(sv), prev["sv_ot"])
                    if prev.get("sa_sq") is not None:
                        t.wait_ge(pr.sem(sa), prev["sa_sq"])
                    if prev.get("sv_sm") is not None:
                        t.wait_ge(pr.sem(sv), prev["sv_sm"])
                pr.inc(t.matmul(repP1[:, :], C["repl2_bf"][0:2, :], mrowb[:, :],
                                start=True, stop=True), spe, 1)
                peb = [None]

                def nopif(bse):
                    if peb[0] != bse:
                        t.nop()
                    peb[0] = bse
                for h in range(NH6):
                    c, jj = h // 2, h % 2
                    dst = oPS[64 * (h % 2):64 * (h % 2) + 64, 64 * (h // 2):64 * (h // 2) + 64]
                    if ci > 0:
                        nopif(32 * jj)
                        t.matmul(dst, s0b[c][32 * jj:32 * jj + 32, :],
                                 rhg[32 * jj:32 * jj + 32, 64 * c:64 * c + 64],
                                 start=True, stop=False)
                        nopif(0)
                    mmx = t.matmul(dst, vt[par][0:64, 64 * h:64 * h + 64],
                                   scm[0:64, 64 * h:64 * h + 64],
                                   start=(ci == 0), stop=True)
                pr.inc(mmx, spe, 1)
            tk["spe_omm"] = None if cstage < 3 else pr.val(spe)

            @B3.vector
            def _(v, par=par, ok=tk.get("spe_omm")):
                v.wait_ge(pr.sem(spe), ok)
                v.tensor_mul(out=tvd[:, :], in0=vfg[par][:, :], in1=repP1[:, :])
                pr.inc(v.tensor_add(out=ot[:, :], in0=oPS[:, :], in1=tvd[:, :]), sv, 1)
            tk["sv_ot"] = None if cstage < 3 else pr.val(sv)

            @B4.scalar
            def _(scl, ok=tk.get("sv_ot")):
                scl.wait_ge(pr.sem(sv), ok)
                pr.inc(scl.activation(out=sq[:, :], in_=ot[:, :], func=AF.Square), sa, 1)
            tk["sa_sq"] = None if cstage < 4 else pr.val(sa)

            @B4.tensor
            def _(t, otk=tk.get("sv_ot"), sqk=tk.get("sa_sq"), prev=prev):
                t.wait_ge(pr.sem(sv), otk)
                t.wait_ge(pr.sem(sa), sqk)
                if prev is not None:
                    t.wait_ge(pr.sem(sv), prev["sv_var"])
                t.matmul(stats[0:2, :], C["hsel2_bf"][:, :], ot[:, :], start=True, stop=True)
                pr.inc(t.matmul(stats[32:34, :], C["hsel2_bf"][:, :], sq[:, :],
                                start=True, stop=True), spe, 1)
            tk["spe_st"] = None if cstage < 4 else pr.val(spe)

            @B4.vector
            def _(v, stk=tk.get("spe_st")):
                v.wait_ge(pr.sem(spe), stk)
                v.tensor_scalar_mul(nsb[:, 0:512], stats[0:2, :], 1.0 / 64.0)
                pr.selfwait(v, v.tensor_scalar_mul(nsb[:, 512:1024], stats[32:34, :], 1.0 / 64.0))
                pr.selfwait(v, v.tensor_mul(out=nsb[:, 1024:1536], in0=nsb[:, 0:512], in1=nsb[:, 0:512]))
                pr.inc(v.tensor_sub(out=nsb[:, 1536:2048], in0=nsb[:, 512:1024],
                                    in1=nsb[:, 1024:1536]), sv, 1)
            tk["sv_var"] = None if cstage < 4 else pr.val(sv)

            @B4.scalar
            def _(scl, vk=tk.get("sv_var")):
                scl.wait_ge(pr.sem(sv), vk)
                scl.activation(out=lnb[:, :], in_=nsb[:, 1536:2048], func=AF.Ln,
                               bias=C["eps_col"][0:2, :])
                scl.activation(out=rsf[:, :], in_=lnb[:, :], func=AF.Exp, scale=-0.5)
                pr.inc(scl.activation(out=rsb[:, :], in_=rsf[:, :], func=AF.Identity), sa, 1)
            tk["sa_rs"] = None if cstage < 4 else pr.val(sa)

            @B4.vector
            def _(v, rk=tk.get("sa_rs")):
                v.wait_ge(pr.sem(sa), rk)
                pr.inc(v.tensor_tensor(out=msb[:, :], in0=nsb[:, 0:512], in1=rsf[:, :],
                                       op=OP.mult), sv, 1)
            tk["sv_ms"] = None if cstage < 4 else pr.val(sv)

            @B4.tensor
            def _(t, mk=tk.get("sv_ms"), prev=prev):
                t.wait_ge(pr.sem(sv), mk)
                if prev is not None:
                    t.wait_ge(pr.sem(sv), prev["sv_fin"])
                t.matmul(repP1[:, :], C["repl2_bf"][0:2, :], rsb[:, :], start=True, stop=True)
                pr.inc(t.matmul(repP2[:, :], C["repl2_bf"][0:2, :], msb[:, :],
                                start=True, stop=True), spe, 1)
            tk["spe_rep"] = None if cstage < 4 else pr.val(spe)

            @B4.vector
            def _(v, par=par, rk=tk.get("spe_rep"), prev2=prev2):
                v.wait_ge(pr.sem(spe), rk)
                if prev2 is not None:
                    v.wait_ge(pr.sem(st), prev2["st"])
                v.tensor_mul(out=fbuf[:, :], in0=ot[:, :], in1=repP1[:, :])
                v.tensor_sub(out=fbuf[:, :], in0=fbuf[:, :], in1=repP2[:, :])
                v.tensor_mul(out=fbuf[:, :], in0=fbuf[:, :], in1=gnwx[:, :])
                v.tensor_add(out=fbuf[:, :], in0=fbuf[:, :], in1=gnbx[:, :])
                pr.inc(v.tensor_mul(out=obf[par][:, :], in0=fbuf[:, :], in1=gsl[par][:, :]), sv, 1)
            tk["sv_fin"] = None if cstage < 4 else pr.val(sv)

            if cstage < 4:
                gate_key = {1: "sv_p2", 2: "sv_scm", 3: "sv_ot"}[cstage]
                gtv = tk[gate_key]

                @blk.vector
                def _(v, par=par, gtv=gtv, prev2=prev2):
                    v.wait_ge(pr.sem(sv), gtv)
                    if prev2 is not None:
                        v.wait_ge(pr.sem(st), prev2["st"])
                    pr.inc(v.tensor_copy(out=obf[par][:, :], in_=gsl[par][:, :]), sv, 1)
                tk["sv_fin"] = pr.val(sv)

            @blk.sync
            def _(sy, par=par, t0=t0, fk=tk["sv_fin"]):
                sy.wait_ge(pr.sem(sv), fk)
                for j in range(8):
                    pr.inc(sy.dma_start(out=o_hbm[j * 128:(j + 1) * 128, t0:t0 + 64],
                                        in_=obf[par][:, j * 64:(j + 1) * 64]), st, 16)
            tk["st"] = pr.val(st)

            @B5.tensor
            def _(t, prev=prev):
                if prev is not None:
                    t.wait_ge(pr.sem(sv), prev["sv_kh"])
                for c in range(8):
                    pr.inc(t.transpose(ktps[0:64, 64 * c:64 * c + 64],
                                       khg[:, 64 * c:64 * c + 64], C["ident_bf"][0:64, 0:64]), spe, 1)
            tk["spe_kt"] = None if cstage < 5 else pr.val(spe)

            @B5.vector
            def _(v, kk=tk.get("spe_kt")):
                v.wait_ge(pr.sem(spe), kk)
                pr.inc(v.tensor_copy(out=khT[:, :], in_=ktps[:, :]), sv, 1)
            tk["sv_kh"] = None if cstage < 5 else pr.val(sv)

            @B5.tensor
            def _(t, par=par, kk=tk.get("sv_kh"), prev=prev):
                t.wait_ge(pr.sem(sv), kk)
                if prev is not None:
                    t.wait_ge(pr.sem(sv), prev["sv_sm"])
                for h in range(NH6):
                    c, jj = h // 2, h % 2
                    pr.inc(t.matmul(Sps[32 * jj:32 * jj + 32, 64 * c:64 * c + 64],
                                    khT[0:64, 64 * c + 32 * jj:64 * c + 32 * jj + 32],
                                    vt[par][0:64, 64 * h:64 * h + 64],
                                    start=True, stop=True), spe, 1)
            tk["spe_smm"] = None if cstage < 5 else pr.val(spe)
            tk["spe_end"] = pr.val(spe)

            @B5.vector
            def _(v, par=par, sk=tk.get("spe_smm")):
                v.wait_ge(pr.sem(spe), sk)
                for c in range(8):
                    ii = v.scalar_tensor_tensor(out=s0f[par][c][:, :], in0=s0f[1 - par][c][:, :],
                                                scalar=ealg[:, c:c + 1], in1=Sps[:, 64 * c:64 * c + 64],
                                                op0=OP.mult, op1=OP.add)
                pr.selfwait(v, ii)
                for c in range(8):
                    ii = v.tensor_copy(out=s0b[c][:, :], in_=s0f[par][c][:, :])
                pr.inc(ii, sv, 1)
            tk["sv_sm"] = None if cstage < 5 else pr.val(sv)
            tk["sv_end"] = pr.val(sv)

            prev2 = prev
            prev = tk

    @blk.sync
    def _(sy):
        pr.wait(sy, st)
    pr.set_done([(st, pr.val(st))])
    sc_.close()


def build_l0_attn_a(pr, C, io, x_in, x_out, scr, ntok=NTOK, nseq=B, tlen=T):
    stage = 3
    u = scr["u"]
    slab = min(1024, ntok)
    emit_ln(pr, C, x_in, scr["xhat"], "al" + u, ntok=ntok, slab=slab)
    emit_shift_mix(pr, C, scr["xhat"], io["mux"], scr["dl"], scr["xm"], "as" + u,
                   ntok=ntok, nseq=nseq, tlen=tlen)
    emit_gemm(pr, C, "ax" + u, scr["xm"], io["wx1"], scr["lowT"], D, 256, ntok=ntok,
              epilogue="act_bias_bf", bias_hbm=io["zc2"], act=AF.Tanh)
    emit_lora5(pr, C, scr["lowT"], scr["dl"], scr["xhat"], io["wx2"], io["xb5"],
               [scr["xr"], scr["xw"], scr["xk"], scr["xv"], scr["xg"]], "a5" + u, ntok=ntok)
    if stage == 1:
        emit_gemm(pr, C, "ao" + u, scr["xr"], io["wo"], None, D, D, ntok=ntok,
                  epilogue="bias_resid_f32", bias_row_hbm=io["zrow"],
                  resid_hbm=x_in, resout_hbm=x_out)
        return
    emit_gemm(pr, C, "ar" + u, scr["xr"], io["wr"], scr["rT"], D, KEYD, ntok=ntok,
              epilogue="copy_f32")
    emit_gemm(pr, C, "aw" + u, scr["xw"], io["ww1"], scr["t1T"], D, 128, ntok=ntok,
              epilogue="act_bias_bf", bias_hbm=io["zc1"], act=AF.Tanh)
    emit_gemm(pr, C, "ae" + u, scr["t1T"], io["ww2"], scr["eT"], 128, KEYD, ntok=ntok,
              epilogue="act_bias_f32", bias_hbm=io["bw2c"], act=AF.Exp)
    emit_gemm(pr, C, "ak" + u, scr["xk"], io["wk"], scr["kT"], D, KEYD, ntok=ntok,
              epilogue="copy_f32")
    emit_gemm_tm(pr, C, "av" + u, scr["xv"], io["wv"], io["zrow"], scr["vtm"], D, D, ntok=ntok)
    emit_gemm(pr, C, "af" + u, scr["xv"], io["wvl"], scr["v_fm"], D, D, ntok=ntok,
              epilogue="copy_bf")
    emit_gemm(pr, C, "ag" + u, scr["xg"], io["wg"], scr["gslT"], D, D, ntok=ntok,
              epilogue="act_bias_bf", bias_hbm=io["zc8"], act=AF.Silu)
    if stage == 2:
        emit_gemm(pr, C, "ao" + u, scr["gslT"], io["wo"], None, D, D, ntok=ntok,
                  epilogue="bias_resid_f32", bias_row_hbm=io["zrow"],
                  resid_hbm=x_in, resout_hbm=x_out)
        return
    emit_rwkv6_core(pr, C, io, scr["eT"], scr["rT"], scr["kT"], scr["vtm"],
                    scr["v_fm"], scr["gslT"], scr["o_fm"], "ac" + u, nseq=nseq, tlen=tlen)
    emit_gemm(pr, C, "ao" + u, scr["o_fm"], io["wo"], None, D, D, ntok=ntok,
              epilogue="bias_resid_f32", bias_row_hbm=io["zrow"],
              resid_hbm=x_in, resout_hbm=x_out)


def rwkv_scratch(pr, ntok=NTOK, u="0"):
    scr = {"u": u}
    for n in ["xhat", "dl", "xm", "xr", "xw", "xk", "xv", "xg", "gslT", "o_fm", "v_fm"]:
        scr[n] = pr.dram("sa_" + n + u, [D, ntok], BF16, kind="Internal")
    scr["lowT"] = pr.dram("sa_lowT" + u, [256, ntok], BF16, kind="Internal")
    scr["t1T"] = pr.dram("sa_t1T" + u, [128, ntok], BF16, kind="Internal")
    for n in ["rT", "kT", "eT"]:
        scr[n] = pr.dram("sa_" + n + u, [KEYD, ntok], F32, kind="Internal")
    scr["vtm"] = pr.dram("sa_vtm" + u, [ntok, D], BF16, kind="Internal")
    return scr



NE_ = NE
HID_ = HID


def _mkprog(build_fn, io_decl, consts, scratch_fn, extra=None):
    pr = Prog()
    x_in = pr.dram("x_in", [D, NTOK], F32, kind="ExternalInput")
    x_out = pr.dram("x_out", [D, NTOK], F32, kind="ExternalOutput")
    io = io_decl(pr)
    cin = {k: pr.dram("c_" + k, list(np.asarray(v).shape), const_dtype(k), kind="ExternalInput")
           for k, v in consts.items()}
    scr = scratch_fn(pr)
    with pr.nc.Block() as blk:
        pr.block = blk
        C = emit_consts(pr, cin)
        build_fn(pr, C, io, x_in, x_out, scr)
    pr.close()
    return pr


def _consts_c(consts):
    return {"c_" + k: np.ascontiguousarray(v) for k, v in consts.items()}


def _prog_l0a():
    consts = dict(consts_host()); consts.update(rwkv_consts_host())

    def bf(pr, C, io, x_in, x_out, scr):
        build_l0_attn_a(pr, C, io, x_in, x_out, scr, ntok=NTOK, nseq=B, tlen=T)
        emit_program_end(pr, ["ao0st"])
    return _mkprog(bf, rwkv_io_dram, consts, lambda pr: rwkv_scratch(pr)), consts


def _host_l0a(inp, consts):
    hio = rwkv_host_io(inp["L0_attn_a_W_x1"], inp["L0_attn_a_W_x2"], inp["L0_attn_a_x_bias"],
                       inp["L0_attn_a_mu_x"], inp["L0_attn_a_W_r"], inp["L0_attn_a_W_w1"],
                       inp["L0_attn_a_W_w2"], inp["L0_attn_a_b_w2"], inp["L0_attn_a_W_k"],
                       inp["L0_attn_a_W_v"], inp["L0_attn_a_W_g"], inp["L0_attn_a_u"],
                       inp["L0_attn_a_gn_w"], inp["L0_attn_a_gn_b"], inp["L0_attn_a_W_o"])
    im = {"a_" + k: v for k, v in hio.items()}
    im.update(_consts_c(consts))
    return im


def _prog_moe():
    mc = moe_consts_host(B, T)
    consts = dict(consts_host()); consts["ident_f32"] = mc["ident_f32"]
    slotp = moe_slotp(B, T)

    def iod(pr):
        io = {"rwT": pr.dram("rwT", [D, NE_], F32, kind="ExternalInput"),
              "w1h": pr.dram("w1h", [NE_, 128, KD * HID_], F32, kind="ExternalInput"),
              "b1c": pr.dram("b1c", [NE_, 128, HID_ // 128], F32, kind="ExternalInput"),
              "w2n": pr.dram("w2n", [NE_, 128, (HID_ // 128) * D], F32, kind="ExternalInput"),
              "b2r": pr.dram("b2r", [NE_, 1, D], F32, kind="ExternalInput")}
        pr.moe_rowoff0 = pr.dram("rowoff0", [64, 1], F32, kind="ExternalInput")
        pr.moe_rowoff1 = pr.dram("rowoff1", [64, 1], F32, kind="ExternalInput")
        return io

    def scrf(pr):
        return {"u": "0",
                "xhat": pr.dram("s_xhat", [D, NTOK], BF16, kind="Internal"),
                "xtm": pr.dram("s_xtm", [NTOK, D], BF16, kind="Internal"),
                "staging": [pr.dram(f"s_stg{e}", [slotp, D], BF16, kind="Internal") for e in range(NE_)],
                "y_scr": [pr.dram(f"s_ysc{e}", [slotp, D], F32, kind="Internal") for e in range(NE_)],
                "idxd": pr.dram("s_idxd", [NTOK // 128, 128, NE_], I32, kind="Internal"),
                "moe_tm": pr.dram("s_mtm", [NTOK, D], F32, kind="Internal")}

    def bf(pr, C, io, x_in, x_out, scr):
        build_l0_ffn(pr, C, io, x_in, x_out, scr, ntok=NTOK, nseq=B, tlen=T)
        emit_program_end(pr, ["ma0st"])
    return _mkprog(bf, iod, consts, scrf), consts, mc


def _host_moe(inp, consts, mc):
    KH = HID_ // 128
    w1, b1, w2, b2 = (np.asarray(inp["L0_ffn_w1"]), np.asarray(inp["L0_ffn_b1"]),
                      np.asarray(inp["L0_ffn_w2"]), np.asarray(inp["L0_ffn_b2"]))
    im = {"rwT": np.ascontiguousarray(np.asarray(inp["L0_ffn_router_w"]).T),
          "w1h": np.stack([lhsT_tiles(w1[e]) for e in range(NE_)]),
          "b1c": np.stack([bias_cols(b1[e]) for e in range(NE_)]),
          "w2n": np.stack([np.ascontiguousarray(
              w2[e].reshape(KH, 128, D).transpose(1, 0, 2).reshape(128, KH * D)) for e in range(NE_)]),
          "b2r": np.ascontiguousarray(b2.reshape(NE_, 1, D)),
          "rowoff0": mc["rowoff0"], "rowoff1": mc["rowoff1"]}
    im.update(_consts_c(consts))
    return im


def _prog_l0b():
    consts = dict(consts_host())
    ac = attn_consts_host(T)
    for k in ["maskb0_bf", "maskb1_bf", "maskb2_bf", "maskb3_bf",
              "maska0_bf", "maska1_bf", "maska2_bf", "maska3_bf",
              "iotacols", "qiota16", "c5ln2"]:
        consts[k] = ac[k]

    def iod(pr):
        io = {n: pr.dram(n, [128, 8 * 1024], F32, kind="ExternalInput")
              for n in ["wq", "wqp", "wk", "wkp", "wg", "wo"]}
        for n in ["bq", "bqp", "bk", "bkp", "bor"]:
            io[n] = pr.dram(n, [1, D], F32, kind="ExternalInput")
        io["bgc"] = pr.dram("bgc", [128, KD], F32, kind="ExternalInput")
        io["wvx"] = pr.dram("wvx", [D, VW], F32, kind="ExternalInput")
        io["bvx"] = pr.dram("bvx", [1, VW], F32, kind="ExternalInput")
        io["wdT"] = pr.dram("wdT", [D, NH], F32, kind="ExternalInput")
        io["bdr"] = pr.dram("bdr", [1, NH], F32, kind="ExternalInput")
        io["gnw"] = pr.dram("gnw", [128, KD], F32, kind="ExternalInput")
        io["gnb"] = pr.dram("gnb", [128, KD], F32, kind="ExternalInput")
        for n in ["ropecosq", "ropesinq", "ropecosk", "ropesink"]:
            io[n] = pr.dram(n, [D, T], F32, kind="ExternalInput")
        pr.attn_hsel_hbm = pr.dram("hselc", [KD * 128, NH], BF16, kind="ExternalInput")
        pr.attn_hselT_hbm = pr.dram("hselTc", [KD * NH, 128], F32, kind="ExternalInput")
        return io

    def scrf(pr):
        scr = {"u": "0"}
        for n in ["xhat", "qc", "qp", "qr", "kr", "g", "o", "og"]:
            scr[n] = pr.dram("s_" + n, [D, NTOK], BF16, kind="Internal")
        scr["vtm"] = pr.dram("s_vtm", [NTOK, VW], BF16, kind="Internal")
        scr["ld_scr"] = pr.dram("s_ld", [B, NH], F32, kind="Internal")
        scr["eq_scr"] = pr.dram("s_eq", [B, NH, T], F32, kind="Internal")
        scr["stat_scr"] = pr.dram("s_st", [B, 1, 2 * NH], F32, kind="Internal")
        return scr

    def bf(pr, C, io, x_in, x_out, scr):
        build_l0_attn_b(pr, C, io, x_in, x_out, scr, ntok=NTOK, nseq=B, tlen=T)
        emit_program_end(pr, ["op0st"])
    return _mkprog(bf, iod, consts, scrf), consts, ac


def _host_l0b(inp, consts, ac):
    wq, bq = np.asarray(inp["L0_attn_b_wq"]), np.asarray(inp["L0_attn_b_bq"])
    wk, bk = np.asarray(inp["L0_attn_b_wk"]), np.asarray(inp["L0_attn_b_bk"])
    wv, bv = np.asarray(inp["L0_attn_b_wv"]), np.asarray(inp["L0_attn_b_bv"])
    wg, bg = np.asarray(inp["L0_attn_b_wg"]), np.asarray(inp["L0_attn_b_bg"])
    wo, bo = np.asarray(inp["L0_attn_b_wo"]), np.asarray(inp["L0_attn_b_bo"])
    wd, bd = np.asarray(inp["L0_attn_b_wd"]), np.asarray(inp["L0_attn_b_bd"])
    im = {"wq": lhsT_tiles(np.ascontiguousarray(wq.T)),
          "wqp": lhsT_tiles(perm_rope(np.ascontiguousarray(wq.T))),
          "bq": np.ascontiguousarray(bq.reshape(1, D)), "bqp": perm_rope_bias(bq),
          "wk": lhsT_tiles(np.ascontiguousarray(wk.T)),
          "wkp": lhsT_tiles(perm_rope(np.ascontiguousarray(wk.T))),
          "bk": np.ascontiguousarray(bk.reshape(1, D)), "bkp": perm_rope_bias(bk),
          "wg": lhsT_tiles(np.ascontiguousarray(wg.T)), "bgc": bias_cols(bg),
          "wo": lhsT_tiles(np.ascontiguousarray(wo.T)),
          "bor": np.ascontiguousarray(bo.reshape(1, D)),
          "wdT": np.ascontiguousarray(wd.T), "bdr": np.ascontiguousarray(bd.reshape(1, NH)),
          "gnw": bias_cols(np.asarray(inp["L0_attn_b_gamma"])),
          "gnb": bias_cols(np.asarray(inp["L0_attn_b_beta"])),
          "ropecosq": ac["ropecos"] / 8.0, "ropesinq": ac["ropesin"] / 8.0,
          "ropecosk": ac["ropecos"], "ropesink": ac["ropesin"],
          "hselc": ac["headsel_bf"], "hselTc": ac["headselT"]}
    im["wvx"], im["bvx"] = vw_ext(np.ascontiguousarray(wv.T), bv)
    im.update(_consts_c(consts))
    return im


def _prog_l1a():
    consts = dict(consts_host())

    def iod(pr):
        return {"bw": pr.dram("bw", [128, 8 * 1024], F32, kind="ExternalInput"),
                "at": pr.dram("at", [128, 8 * 1024], F32, kind="ExternalInput"),
                "cw": pr.dram("cw", [128, 8 * 1024], F32, kind="ExternalInput"),
                "dc": pr.dram("dc", [128, KD], F32, kind="ExternalInput")}

    def scrf(pr):
        return {"u": "0",
                "xhat": pr.dram("s_xhat", [D, NTOK], BF16, kind="Internal"),
                "bx": pr.dram("s_bx", [D, NTOK], F32, kind="Internal"),
                "hall": pr.dram("s_hall", [D, NTOK], BF16, kind="Internal"),
                "resid2": pr.dram("s_r2", [D, NTOK], F32, kind="Internal")}

    def bf(pr, C, io, x_in, x_out, scr):
        build_l1_attn_a(pr, C, io, x_in, x_out, scr)
        emit_program_end(pr, ["gy0st"])
    return _mkprog(bf, iod, consts, scrf), consts


def _host_l1a(inp, consts):
    im = {"bw": lhsT_tiles(np.ascontiguousarray(np.asarray(inp["L1_attn_a_B_w"]).T)),
          "at": lhsT_tiles(np.ascontiguousarray(np.asarray(inp["L1_attn_a_A"]).T)),
          "cw": lhsT_tiles(np.ascontiguousarray(np.asarray(inp["L1_attn_a_C_w"]).T)),
          "dc": bias_cols(np.asarray(inp["L1_attn_a_D"]))}
    im.update(_consts_c(consts))
    return im


def _prog_l1f():
    consts = dict(consts_host())

    def iod(pr):
        return {"w1": pr.dram("w1", [128, KD * HID_], F32, kind="ExternalInput"),
                "b1c": pr.dram("b1c", [128, HID_ // 128], F32, kind="ExternalInput"),
                "w2": pr.dram("w2", [128, (HID_ // 128) * D], F32, kind="ExternalInput"),
                "b2r": pr.dram("b2r", [1, D], F32, kind="ExternalInput")}

    def scrf(pr):
        return {"u": "0",
                "xhat": pr.dram("s_xhat", [D, NTOK], BF16, kind="Internal"),
                "h4k": pr.dram("s_h4k", [HID_, NTOK], BF16, kind="Internal")}

    def bf(pr, C, io, x_in, x_out, scr):
        build_l1_ffn(pr, C, io, x_in, x_out, scr)
        emit_program_end(pr, ["f20st"])
    return _mkprog(bf, iod, consts, scrf), consts


def _host_l1f(inp, consts):
    im = {"w1": lhsT_tiles(np.asarray(inp["L1_ffn_w1"])),
          "b1c": bias_cols(np.asarray(inp["L1_ffn_b1"])),
          "w2": lhsT_tiles(np.asarray(inp["L1_ffn_w2"])),
          "b2r": np.ascontiguousarray(np.asarray(inp["L1_ffn_b2"]).reshape(1, D))}
    im.update(_consts_c(consts))
    return im


def _prog_l1b():
    consts = dict(consts_host())

    def iod(pr):
        io = {n: pr.dram(n, [128, 8 * 1024], F32, kind="ExternalInput")
              for n in ["wr", "wk", "wv", "wo"]}
        io["brc"] = pr.dram("brc", [128, KD], F32, kind="ExternalInput")
        for n in ["bkr", "bvr", "bor"]:
            io[n] = pr.dram(n, [1, D], F32, kind="ExternalInput")
        io["tdc"] = pr.dram("tdc", [128, KD], F32, kind="ExternalInput")
        io["uc"] = pr.dram("uc", [128, KD], F32, kind="ExternalInput")
        return io

    def scrf(pr):
        scr = {"u": "0"}
        for n in ["xhat", "xir", "xik", "xiv", "sigr", "wo_in"]:
            scr[n] = pr.dram("s_" + n, [D, NTOK], BF16, kind="Internal")
        for n in ["kf", "vf"]:
            scr[n] = pr.dram("s_" + n, [D, NTOK], F32, kind="Internal")
        return scr

    def bf(pr, C, io, x_in, x_out, scr):
        build_l1_attn_b(pr, C, io, x_in, x_out, scr)
        emit_program_end(pr, ["go0st"])
    return _mkprog(bf, iod, consts, scrf), consts


def _host_l1b(inp, consts):
    im = {"wr": lhsT_tiles(np.ascontiguousarray(np.asarray(inp["L1_attn_b_Wr"]).T)),
          "brc": bias_cols(np.asarray(inp["L1_attn_b_br"])),
          "wk": lhsT_tiles(np.ascontiguousarray(np.asarray(inp["L1_attn_b_Wk"]).T)),
          "bkr": np.ascontiguousarray(np.asarray(inp["L1_attn_b_bk"]).reshape(1, D)),
          "wv": lhsT_tiles(np.ascontiguousarray(np.asarray(inp["L1_attn_b_Wv"]).T)),
          "bvr": np.ascontiguousarray(np.asarray(inp["L1_attn_b_bv"]).reshape(1, D)),
          "wo": lhsT_tiles(np.ascontiguousarray(np.asarray(inp["L1_attn_b_Wo"]).T)),
          "bor": np.ascontiguousarray(np.asarray(inp["L1_attn_b_bo"]).reshape(1, D)),
          "tdc": bias_cols(np.asarray(inp["L1_attn_b_time_decay"])),
          "uc": bias_cols(np.asarray(inp["L1_attn_b_U"]))}
    im.update(_consts_c(consts))
    return im


def kernel(**inputs):
    x = np.asarray(inputs["L0_attn_a_x"], np.float32)
    shards = [to_fm(x[4 * i:4 * i + 4]) for i in range(NCORES)]

    p_l0a, c_l0a = _prog_l0a()
    p_moe, c_moe, mc_moe = _prog_moe()
    p_l0b, c_l0b, ac_l0b = _prog_l0b()
    p_l1a, c_l1a = _prog_l1a()
    p_l1f, c_l1f = _prog_l1f()
    p_l1b, c_l1b = _prog_l1b()

    h_l0a = _host_l0a(inputs, c_l0a)
    h_moe = _host_moe(inputs, c_moe, mc_moe)
    h_l0b = _host_l0b(inputs, c_l0b, ac_l0b)
    h_l1a = _host_l1a(inputs, c_l1a)
    h_l1f = _host_l1f(inputs, c_l1f)
    h_l1b = _host_l1b(inputs, c_l1b)

    chain = [(p_l0a, h_l0a), (p_moe, h_moe), (p_l0b, h_l0b), (p_moe, h_moe),
             (p_l1a, h_l1a), (p_l1f, h_l1f), (p_l1b, h_l1b), (p_l1f, h_l1f)]
    cur = shards
    for pr, host in chain:
        in_maps = [{**host, "x_in": cur[i]} for i in range(NCORES)]
        res = run_bass_kernel_spmd(pr.nc, in_maps, core_ids=list(range(NCORES)))
        cur = [np.asarray(res.results[i]["x_out"], np.float32) for i in range(NCORES)]
    return np.concatenate([from_fm(c, B, T) for c in cur], 0)
